# version 01 (final)
# baseline (speedup 1.0000x reference)
import math
import numpy as np
import ml_dtypes
import concourse.bass as bass
import concourse.mybir as mybir
from concourse.bass_utils import run_bass_kernel_spmd

F32 = mybir.dt.float32
BF16 = mybir.dt.bfloat16
I32 = mybir.dt.int32
I8 = mybir.dt.int8
AF = mybir.ActivationFunctionType
ALU = mybir.AluOpType
BF = ml_dtypes.bfloat16

D_MODEL = 1024
BATCH = 4
SEQ = 4096
NTOK = BATCH * SEQ

RND_C = 3.0 * 2.0**22


ENGS = ("tensor", "vector", "scalar", "gpsimd", "sync")


class H:
    __slots__ = ("key", "cnt")

    def __init__(self, key, cnt):
        self.key = key
        self.cnt = cnt


class Em:

    def __init__(self, nc):
        self.nc = nc
        self.sem = {e: nc.alloc_semaphore(f"S_{e}") for e in ENGS}
        self.cnt = {e: 0 for e in ENGS}
        self.ops = {e: [] for e in ENGS}

    def chan(self, name):
        key = "ch_" + name
        if key not in self.sem:
            self.sem[key] = self.nc.alloc_semaphore(key)
            self.cnt[key] = 0
        return key

    def op(self, eng, fn, deps=(), inc=True):
        waits = self._waits(eng, deps)
        amt = 1 if inc else 0
        self.ops[eng].append((waits, fn, amt, eng))
        if amt:
            self.cnt[eng] += amt
            return H(eng, self.cnt[eng])
        return None

    @staticmethod
    def _waits(eng, deps):
        waits = {}
        for d in deps:
            if d is None:
                continue
            waits[d.key] = max(waits.get(d.key, 0), d.cnt)
        return waits

    def dma(self, eng, out, in_, deps=(), chan="misc", **kw):
        key = self.chan(chan)

        def fn(E):
            return E.dma_start(out=out, in_=in_, **kw)
        waits = self._waits(eng, deps)
        self.ops[eng].append((waits, fn, 16, key))
        self.cnt[key] += 16
        return H(key, self.cnt[key])

    def barrier(self, eng, deps):
        waits = self._waits(eng, deps)
        if waits:
            self.ops[eng].append((waits, None, 0, eng))

    def emit(self):
        nc = self.nc
        with nc.Block() as block:
            def mk(eng_name):
                def run(E):
                    seen = {}
                    for waits, fn, amt, key in self.ops[eng_name]:
                        for p, v in waits.items():
                            if v > seen.get(p, 0):
                                E.wait_ge(self.sem[p], v)
                                seen[p] = v
                        if fn is None:
                            continue
                        ins = fn(E)
                        if amt:
                            if isinstance(ins, list):
                                ins = ins[-1]
                            ins.then_inc(self.sem[key], amt)
                return run
            if self.ops["sync"]:
                block.sync(mk("sync"))
            if self.ops["gpsimd"]:
                block.gpsimd(mk("gpsimd"))
            if self.ops["tensor"]:
                block.tensor(mk("tensor"))
            if self.ops["vector"]:
                block.vector(mk("vector"))
            if self.ops["scalar"]:
                block.scalar(mk("scalar"))



def ceil_div(a, b):
    return (a + b - 1) // b


class Lib:

    def __init__(self, em: Em):
        self.em = em
        self.nc = em.nc
        self._uid = 0
        self._consts = {}

    def uid(self):
        self._uid += 1
        return self._uid

    def sb(self, shape, dtype, name=None):
        name = name or f"t{self.uid()}"
        return self.nc.alloc_sbuf_tensor(name, shape, dtype)

    def const(self, arr, name=None):
        key = name
        if key is not None and key in self._consts:
            return self._consts[key]
        name = name or f"c{self.uid()}"
        arr = np.ascontiguousarray(arr)
        d = self.nc.inline_tensor(arr, name + "_d").ap()
        t = self.nc.alloc_sbuf_tensor(name, list(arr.shape), np2dt(arr.dtype))
        h = self.em.dma("sync", t[:, :] if len(arr.shape) == 2 else t[:],
                        d[:, :] if len(arr.shape) == 2 else d[:], chan="c_" + name)
        if key is not None:
            self._consts[key] = (t, h)
        return t, h

    def ones_row(self):
        return self.const(np.ones((1, 128), BF), "ones_row")

    def ones_col(self):
        return self.const(np.ones((128, 1), BF), "ones_col")

    def ident128(self):
        return self.const(np.eye(128, dtype=BF), "ident128")

    def ident128f(self):
        return self.const(np.eye(128, dtype=np.float32), "ident128f")

    def ones_rowf(self):
        return self.const(np.ones((1, 128), np.float32), "ones_rowf")

    def tril128(self, strict=False, name=None):
        m = np.tril(np.ones((128, 128), np.float32), -1 if strict else 0)
        return self.const(m.astype(BF), name or ("tril128s" if strict else "tril128"))

    def triu128(self, strict=False, name=None):
        m = np.triu(np.ones((128, 128), np.float32), 1 if strict else 0)
        return self.const(m.astype(BF), name or ("triu128s" if strict else "triu128"))


def np2dt(dtype):
    return {np.dtype(np.float32): F32, np.dtype(BF): BF16,
            np.dtype(np.int32): I32, np.dtype(np.int8): I8}[np.dtype(dtype)]


def emit_ln_stats(L, x_ap, sq_scratch, col4, deps, eps=1e-5):
    em, D = L.em, x_ap.shape[1]
    mean, rstd, t2, t3 = (col4[:, i:i + 1] for i in range(4))
    h1 = em.op("vector", lambda E: E.reduce_sum(mean, x_ap, axis=mybir.AxisListType.X), deps=deps)
    h2 = em.op("scalar", lambda E: E.activation(sq_scratch, x_ap, AF.Square), deps=deps)
    h3 = em.op("vector", lambda E: E.reduce_sum(t2, sq_scratch, axis=mybir.AxisListType.X), deps=[h2])
    h4 = em.op("vector", lambda E: E.tensor_scalar(mean, mean, 1.0 / D, None, ALU.mult), deps=[h1])
    h5 = em.op("scalar", lambda E: E.activation(t3, mean, AF.Square), deps=[h4])
    h6 = em.op("vector", lambda E: E.tensor_scalar(t2, t2, 1.0 / D, eps, ALU.mult, ALU.add), deps=[h3])
    h7 = em.op("vector", lambda E: E.tensor_sub(t2, t2, t3), deps=[h5, h6])
    h8 = em.op("scalar", lambda E: E.activation(t3, t2, AF.Sqrt), deps=[h7])
    h9 = em.op("vector", lambda E: E.reciprocal(rstd, t3), deps=[h8])
    return h9


def emit_ln_apply(L, x_ap, out_ap, col4, deps):
    em = L.em
    return em.op("vector", lambda E: E.tensor_scalar(
        out_ap, x_ap, col4[:, 0:1], col4[:, 1:2], ALU.subtract, ALU.mult), deps=deps)


def emit_transpose128(L, src_ap, dst_ap, ident, psum, deps, eng="vector"):
    em = L.em
    h = em.op("tensor", lambda E: E.transpose(psum[:, :], src_ap, ident[:, :]), deps=deps)
    return em.op(eng, lambda E: E.tensor_copy(dst_ap, psum[:, :]), deps=[h])



def build_l0b(T=2048):
    nc = bass.Bass("TRN2", target_bir_lowering=False)
    x_d = nc.dram_tensor("x", [T, 1024], F32, kind="ExternalInput").ap()
    yg_d = nc.dram_tensor("yg", [T, 2048], BF16, kind="ExternalInput").ap()
    opw_d = nc.dram_tensor("opw", [2048, 1024], F32, kind="ExternalInput").ap()
    gw_d = nc.dram_tensor("gw", [1024, 4096], F32, kind="ExternalInput").ap()
    uw_d = nc.dram_tensor("uw", [1024, 4096], F32, kind="ExternalInput").ap()
    dw_d = nc.dram_tensor("dw", [4096, 1024], F32, kind="ExternalInput").ap()
    ssmw_d = nc.dram_tensor("ssmw", [1, 2048], F32, kind="ExternalInput").ap()
    postw_d = nc.dram_tensor("postw", [1, 1024], F32, kind="ExternalInput").ap()
    x1_d = nc.dram_tensor("x1", [T, 1024], F32, kind="ExternalOutput").ap()

    em = Em(nc)
    L = Lib(em)
    ident, h_id = L.ident128()
    ones_row, h_or = L.ones_row()

    ps = [nc.alloc_psum_tensor(f"ps{i}", [128, 512], F32) for i in range(7)]
    pt = nc.alloc_psum_tensor("pt", [128, 128], BF16)

    sb = L.sb
    opw_sb = sb([128, 16 * 1024], BF16, "opw_sb")
    gw_sb = [sb([128, 1024], BF16, f"gw{i}") for i in range(2)]
    uw_sb = [sb([128, 1024], BF16, f"uw{i}") for i in range(2)]
    dw_sb = [sb([128, 512], BF16, f"dw{i}") for i in range(2)]
    yg_sb = [sb([128, 2048], BF16, f"yg{i}") for i in range(4)]
    sq2 = sb([128, 2048], BF16, "sq2")
    cols = sb([128, 32], F32, "cols")
    nrm_tm = sb([128, 2048], BF16, "nrm_tm")
    nrm_fm = [sb([128, 512], BF16, f"nfm{k}") for k in range(16)]
    wbc_ssm = sb([128, 2048], BF16, "wbc_ssm")
    wbc_post = sb([128, 1024], BF16, "wbc_post")
    rowbuf = sb([1, 2048], BF16, "rowbuf")
    x_sb = [sb([128, 1024], F32, f"xs{i}") for i in range(4)]
    h_sb = [sb([128, 1024], F32, f"hs{i}") for i in range(4)]
    hn_tm = sb([128, 1024], BF16, "hn_tm")
    hn_fm = [sb([128, 512], BF16, f"hfm{k}") for k in range(8)]
    sg = sb([128, 512], BF16, "sg")
    gu = sb([128, 32 * 512], BF16, "gu")
    o_sb = [sb([128, 1024], F32, f"os{i}") for i in range(4)]

    h_opw = em.dma("gpsimd", opw_sb.ap().rearrange("p (k m) -> p k m", k=16),
                   opw_d.rearrange("(k p) m -> p k m", p=128), chan="opw")
    h_r1 = em.dma("gpsimd", rowbuf[:1, :2048], ssmw_d[:, :], chan="rowbuf")

    def bcast_row(src_row_ap, dst_sb, N, deps):
        hs = []
        for j in range(0, N, 512):
            w = min(512, N - j)
            hm = em.op("tensor", lambda E, j=j, w=w: E.matmul(
                ps[6][:, :w], ones_row[:1, :], src_row_ap[:, j:j + w],
                start=True, stop=True), deps=deps + [h_or])
            hc = em.op("vector", lambda E, j=j, w=w: E.tensor_copy(
                dst_sb[:, j:j + w], ps[6][:, :w]), deps=[hm])
            hs.append(hc)
            deps = deps + [hc]
        return hs

    hb1 = bcast_row(rowbuf[:1, :], wbc_ssm, 2048, [h_r1])
    h_r2 = em.dma("gpsimd", rowbuf[:1, :1024], postw_d[:, :], deps=hb1, chan="rowbuf")
    hb2 = bcast_row(rowbuf[:1, :1024], wbc_post, 1024, [h_r2])

    h_x1_stores = []
    h_gu_done_prev = None

    for tt in range(T // 512):
        t0 = tt * 512
        h_yg, h_x = [], []
        for c in range(4):
            r0 = t0 + c * 128
            h_yg.append(em.dma("sync", yg_sb[c][:, :], yg_d[r0:r0 + 128, :], chan=f"yg{c}"))
            h_x.append(em.dma("sync", x_sb[c][:, :], x_d[r0:r0 + 128, :], chan=f"xx{c}"))

        h_nfm = [[None] * 4 for _ in range(16)]
        h_h = []
        for c in range(4):
            cc = cols[:, c * 8:(c + 1) * 8]
            hsq = em.op("scalar", lambda E, c=c, cc=cc: E.activation(
                sq2[:, :], yg_sb[c][:, :], AF.Square), deps=[h_yg[c]] + ([h_rms_prev] if c > 0 or tt > 0 else []))
            hss = em.op("vector", lambda E, cc=cc: E.reduce_sum(
                cc[:, 0:1], sq2[:, :], axis=mybir.AxisListType.X), deps=[hsq])
            h_rms_prev = hss
            hm1 = em.op("vector", lambda E, cc=cc: E.tensor_scalar(
                cc[:, 1:2], cc[:, 0:1], 1.0 / 2048.0, 1e-6, ALU.mult, ALU.add), deps=[hss])
            hs1 = em.op("scalar", lambda E, cc=cc: E.activation(
                cc[:, 2:3], cc[:, 1:2], AF.Sqrt), deps=[hm1])
            hr1 = em.op("vector", lambda E, cc=cc: E.reciprocal(
                cc[:, 3:4], cc[:, 2:3]), deps=[hs1])
            hap = em.op("vector", lambda E, c=c, cc=cc: E.scalar_tensor_tensor(
                nrm_tm[:, :], yg_sb[c][:, :], cc[:, 3:4], wbc_ssm[:, :],
                ALU.mult, ALU.mult), deps=[hr1] + hb1)
            hprev = hap
            for k in range(16):
                hm = em.op("tensor", lambda E, k=k: E.transpose(
                    pt[:, :], nrm_tm[:, k * 128:(k + 1) * 128], ident[:, :]),
                    deps=[hprev, h_id])
                hc = em.op("vector", lambda E, k=k, c=c: E.tensor_copy(
                    nrm_fm[k][:, c * 128:(c + 1) * 128], pt[:, :]), deps=[hm])
                h_nfm[k][c] = hc
                hprev = hc
            hst = emit_ln_stats(L, x_sb[c][:, :], sq2[:, :1024], cc[:, 4:8],
                                deps=[h_x[c], hprev])
            h_rms_prev = hst
            hln = em.op("vector", lambda E, c=c, cc=cc: E.tensor_scalar(
                h_sb[c][:, :], x_sb[c][:, :], cc[:, 4:5], cc[:, 5:6],
                ALU.subtract, ALU.mult), deps=[hst])
            h_h.append(hln)

        for c in range(4):
            for fh in range(2):
                pj = ps[c]
                deps = [h_nfm[k][c] for k in range(16)] + [h_opw]
                if fh == 1:
                    deps.append(h_hadd)
                def mmfn(E, c=c, fh=fh, pj=pj):
                    ins = None
                    for k in range(16):
                        ins = E.matmul(pj[:, :],
                                       nrm_fm[k][:, c * 128:(c + 1) * 128],
                                       opw_sb[:, k * 1024 + fh * 512: k * 1024 + fh * 512 + 512],
                                       start=(k == 0), stop=(k == 15))
                    return ins
                hmm = em.op("tensor", mmfn, deps=deps)
                h_hadd = em.op("vector", lambda E, c=c, fh=fh, pj=pj: E.tensor_add(
                    h_sb[c][:, fh * 512:(fh + 1) * 512],
                    h_sb[c][:, fh * 512:(fh + 1) * 512], pj[:, :]),
                    deps=[hmm, h_h[c]])
            cc = cols[:, c * 8:(c + 1) * 8]
            hsq = em.op("scalar", lambda E, c=c: E.activation(
                sq2[:, :1024], h_sb[c][:, :], AF.Square), deps=[h_hadd, h_rms_prev])
            hss = em.op("vector", lambda E, cc=cc: E.reduce_sum(
                cc[:, 0:1], sq2[:, :1024], axis=mybir.AxisListType.X), deps=[hsq])
            h_rms_prev = hss
            hm1 = em.op("vector", lambda E, cc=cc: E.tensor_scalar(
                cc[:, 1:2], cc[:, 0:1], 1.0 / 1024.0, 1e-6, ALU.mult, ALU.add), deps=[hss])
            hs1 = em.op("scalar", lambda E, cc=cc: E.activation(
                cc[:, 2:3], cc[:, 1:2], AF.Sqrt), deps=[hm1])
            hr1 = em.op("vector", lambda E, cc=cc: E.reciprocal(
                cc[:, 3:4], cc[:, 2:3]), deps=[hs1])
            hap = em.op("vector", lambda E, c=c, cc=cc: E.scalar_tensor_tensor(
                hn_tm[:, :], h_sb[c][:, :], cc[:, 3:4], wbc_post[:, :],
                ALU.mult, ALU.mult), deps=[hr1] + hb2)
            em.op("vector", lambda E, c=c: E.tensor_add(
                h_sb[c][:, :], h_sb[c][:, :], x_sb[c][:, :]), deps=[hap])
            hprev = hap
            for k in range(8):
                hm = em.op("tensor", lambda E, k=k: E.transpose(
                    pt[:, :], hn_tm[:, k * 128:(k + 1) * 128], ident[:, :]),
                    deps=[hprev, h_id])
                hc = em.op("vector", lambda E, k=k, c=c: E.tensor_copy(
                    hn_fm[k][:, c * 128:(c + 1) * 128], pt[:, :]), deps=[hm])
                hprev = hc
            h_hnfm_c = hprev
        h_hnfm = h_hnfm_c

        h_wload = [None, None]
        h_gu = [None] * 32
        h_gumm_prev = [None, None]
        for ff in range(32):
            b = ff % 2
            deps_w = [h_gumm_prev[b]] if h_gumm_prev[b] else []
            hw1 = em.dma("gpsimd", gw_sb[b].ap().rearrange("p (k f) -> p k f", k=8),
                         gw_d[:, ff * 128:(ff + 1) * 128].rearrange("(k p) f -> p k f", p=128),
                         deps=deps_w, chan=f"gw{b}")
            hw2 = em.dma("gpsimd", uw_sb[b].ap().rearrange("p (k f) -> p k f", k=8),
                         uw_d[:, ff * 128:(ff + 1) * 128].rearrange("(k p) f -> p k f", p=128),
                         deps=deps_w, chan=f"uw{b}")
            def gmm(E, b=b, tt=tt):
                ins = None
                for k in range(8):
                    ins = E.matmul(ps[0][:, :], gw_sb[b][:, k * 128:(k + 1) * 128],
                                   hn_fm[k][:, :], start=(k == 0), stop=(k == 7))
                return ins
            def umm(E, b=b, tt=tt):
                ins = None
                for k in range(8):
                    ins = E.matmul(ps[1][:, :], uw_sb[b][:, k * 128:(k + 1) * 128],
                                   hn_fm[k][:, :], start=(k == 0), stop=(k == 7))
                return ins
            hg = em.op("tensor", gmm, deps=[hw1, h_hnfm] + ([h_sl] if ff > 0 else []))
            hu = em.op("tensor", umm, deps=[hw2] + ([h_gu[ff - 1]] if ff > 0 else []))
            h_sl = em.op("scalar", lambda E: E.activation(
                sg[:, :], ps[0][:, :], AF.Silu),
                deps=[hg] + ([h_gu[ff - 1]] if ff > 0 else []))
            h_gu[ff] = em.op("vector", lambda E, ff=ff: E.tensor_mul(
                gu[:, ff * 512:(ff + 1) * 512], sg[:, :], ps[1][:, :]),
                deps=[h_sl, hu])
            h_gumm_prev[b] = h_gu[ff]

        h_dw_prev = [None, None]
        for dh in range(2):
            h_dmm = [None] * 4
            for ff in range(32):
                b = ff % 2
                hwd = em.dma("gpsimd", dw_sb[b][:, :],
                             dw_d[ff * 128:(ff + 1) * 128, dh * 512:(dh + 1) * 512],
                             deps=[h_dw_prev[b]] if h_dw_prev[b] else [], chan=f"dw{b}")
                for c in range(4):
                    h_dmm[c] = em.op("tensor", lambda E, ff=ff, c=c, b=b, dh=dh: E.matmul(
                        ps[2 + c][:, :], gu[:, ff * 512 + c * 128: ff * 512 + (c + 1) * 128],
                        dw_sb[b][:, :], start=(ff == 0), stop=(ff == 31)),
                        deps=[hwd, h_gu[ff]] + ([h_oadd[c]] if ff == 0 and (dh == 1) else []))
                h_dw_prev[b] = h_dmm[3]
            h_oadd = []
            for c in range(4):
                h_oadd.append(em.op("vector", lambda E, c=c, dh=dh: E.tensor_add(
                    o_sb[c][:, dh * 512:(dh + 1) * 512],
                    h_sb[c][:, dh * 512:(dh + 1) * 512], ps[2 + c][:, :]),
                    deps=[h_dmm[c]]))
        for c in range(4):
            h_x1_stores.append(em.dma(
                "sync", x1_d[t0 + c * 128: t0 + (c + 1) * 128, :], o_sb[c][:, :],
                deps=[h_oadd[c]], chan="st"))
        em.barrier("vector", h_x1_stores[-4:])
        em.barrier("scalar", h_x1_stores[-4:])
        em.barrier("tensor", h_x1_stores[-4:])
        em.barrier("sync", h_x1_stores[-4:])
        em.barrier("gpsimd", h_x1_stores[-4:])

    em.barrier("sync", h_x1_stores)
    em.emit()
    return nc


def run_l0b(p, x_flat, yg_flat):
    nc = build_l0b()
    T = 2048
    in_maps = []
    for c in range(8):
        in_maps.append({
            "x": np.ascontiguousarray(x_flat[c * T:(c + 1) * T]),
            "yg": np.ascontiguousarray(yg_flat[c * T:(c + 1) * T]),
            "opw": p["l0_out_proj_T"], "gw": p["l0_gate_T"],
            "uw": p["l0_up_T"], "dw": p["l0_down_T"],
            "ssmw": p["l0_ssm_norm_w"].reshape(1, -1),
            "postw": p["l0_postnorm_w"].reshape(1, -1),
        })
    res = run_bass_kernel_spmd(nc, in_maps, core_ids=list(range(8)))
    return np.concatenate([r["x1"] for r in res.results], axis=0)



def build_l1b(T=2048):
    nc = bass.Bass("TRN2", target_bir_lowering=False)
    x2_d = nc.dram_tensor("x2", [T, 1024], F32, kind="ExternalInput").ap()
    ogn_d = nc.dram_tensor("ogn", [T, 1536], BF16, kind="ExternalInput").ap()
    wo_d = nc.dram_tensor("wo", [1536, 1024], F32, kind="ExternalInput").ap()
    u1_d = nc.dram_tensor("u1", [1024, 256], F32, kind="ExternalInput").ap()
    v1_d = nc.dram_tensor("v1", [256, 4096], F32, kind="ExternalInput").ap()
    b1_d = nc.dram_tensor("b1c", [128, 32], F32, kind="ExternalInput").ap()
    u2_d = nc.dram_tensor("u2", [4096, 256], F32, kind="ExternalInput").ap()
    v2_d = nc.dram_tensor("v2", [256, 1024], F32, kind="ExternalInput").ap()
    b2_d = nc.dram_tensor("b2r", [1, 1024], F32, kind="ExternalInput").ap()
    x4_d = nc.dram_tensor("x4", [T, 1024], F32, kind="ExternalOutput").ap()

    em = Em(nc)
    L = Lib(em)
    ident, h_id = L.ident128()
    ones_row, h_or = L.ones_row()
    ps = [nc.alloc_psum_tensor(f"ps{i}", [128, 512], F32) for i in range(7)]
    pt = nc.alloc_psum_tensor("pt", [128, 128], BF16)
    sb = L.sb

    wo_sb = sb([128, 12 * 1024], BF16, "wo_sb")
    u1_sb = sb([128, 8 * 256], BF16, "u1_sb")
    v1_sb = sb([128, 2 * 4096], BF16, "v1_sb")
    u2_sb = sb([128, 32 * 256], BF16, "u2_sb")
    v2_sb = sb([128, 2 * 1024], BF16, "v2_sb")
    b1c = sb([128, 32], F32, "b1c_sb")
    rowbuf = sb([1, 1024], BF16, "rb")
    wbc_b2 = sb([128, 1024], F32, "wbc_b2")
    ogn_sb = [sb([128, 1536], BF16, f"ogn{i}") for i in range(4)]
    ogn_fm = [sb([128, 512], BF16, f"ofm{k}") for k in range(12)]
    x3_sb = [sb([128, 1024], F32, f"x3s{i}") for i in range(4)]
    xn_tm = sb([128, 1024], BF16, "xn_tm")
    xn_fm = [sb([128, 512], BF16, f"xnf{k}") for k in range(8)]
    p_fm = [sb([128, 512], BF16, f"pfm{k}") for k in range(2)]
    hh = sb([128, 32 * 512], BF16, "hh")
    q_fm = [sb([128, 512], BF16, f"qfm{k}") for k in range(2)]
    sq2 = sb([128, 1024], BF16, "sq2b")
    cols = sb([128, 32], F32, "colsb")
    o_sb = [sb([128, 1024], F32, f"ob{i}") for i in range(4)]

    def kld(name, dst_sb, dram, K, M):
        return em.dma("gpsimd", dst_sb.ap().rearrange("p (k m) -> p k m", k=K),
                      dram.rearrange("(k p) m -> p k m", p=128), chan=name)

    h_wo = kld("wo", wo_sb, wo_d, 12, 1024)
    h_u1 = kld("u1", u1_sb, u1_d, 8, 256)
    h_v1 = kld("v1", v1_sb, v1_d, 2, 4096)
    h_u2 = kld("u2", u2_sb, u2_d, 32, 256)
    h_v2 = kld("v2", v2_sb, v2_d, 2, 1024)
    h_b1 = em.dma("gpsimd", b1c[:, :], b1_d[:, :], chan="b1")
    h_b2r = em.dma("gpsimd", rowbuf[:1, :], b2_d[:, :], chan="b2")
    hb2 = []
    dep = [h_b2r, h_or]
    for j in range(0, 1024, 512):
        hm = em.op("tensor", lambda E, j=j: E.matmul(
            ps[6][:, :], ones_row[:1, :], rowbuf[:1, j:j + 512],
            start=True, stop=True), deps=dep)
        hc = em.op("vector", lambda E, j=j: E.tensor_copy(
            wbc_b2[:, j:j + 512], ps[6][:, :]), deps=[hm])
        hb2.append(hc)
        dep = dep + [hc]

    h_st = []
    h_last_use = [None] * 4
    h_qdone_prev = None
    for tt in range(4):
        t0 = tt * 512
        h_ogn, h_x2 = [], []
        for c in range(4):
            r0 = t0 + c * 128
            dd = [h_last_use[c]] if h_last_use[c] else []
            h_ogn.append(em.dma("sync", ogn_sb[c][:, :], ogn_d[r0:r0 + 128, :],
                                deps=dd, chan=f"ogn{c}"))
            h_x2.append(em.dma("sync", x3_sb[c][:, :], x2_d[r0:r0 + 128, :],
                               deps=dd, chan=f"x2{c}"))
        h_ofm = [[None] * 4 for _ in range(12)]
        hprev_t = h_qdone_prev
        for c in range(4):
            for k in range(12):
                hm = em.op("tensor", lambda E, c=c, k=k: E.transpose(
                    pt[:, :], ogn_sb[c][:, k * 128:(k + 1) * 128], ident[:, :]),
                    deps=[h_ogn[c], h_id] + ([hprev_t] if hprev_t else []))
                hc = em.op("vector", lambda E, c=c, k=k: E.tensor_copy(
                    ogn_fm[k][:, c * 128:(c + 1) * 128], pt[:, :]), deps=[hm])
                h_ofm[k][c] = hc
                hprev_t = hc
        h_x3 = []
        h_xnfm_last = None
        for c in range(4):
            for fh in range(2):
                def mmfn(E, c=c, fh=fh):
                    ins = None
                    for k in range(12):
                        ins = E.matmul(ps[c][:, :],
                                       ogn_fm[k][:, c * 128:(c + 1) * 128],
                                       wo_sb[:, k * 1024 + fh * 512: k * 1024 + (fh) * 512 + 512],
                                       start=(k == 0), stop=(k == 11))
                    return ins
                deps = [h_ofm[k][c] for k in range(12)] + [h_wo]
                if fh == 1:
                    deps.append(h_x3[-1])
                hmm = em.op("tensor", mmfn, deps=deps)
                h_x3.append(em.op("vector", lambda E, c=c, fh=fh: E.tensor_add(
                    x3_sb[c][:, fh * 512:(fh + 1) * 512],
                    x3_sb[c][:, fh * 512:(fh + 1) * 512], ps[c][:, :]),
                    deps=[hmm, h_x2[c]]))
            cc = cols[:, c * 8:(c + 1) * 8]
            hst = emit_ln_stats(L, x3_sb[c][:, :], sq2[:, :], cc,
                                deps=[h_x3[-1], h_xnfm_last])
            hap = emit_ln_apply(L, x3_sb[c][:, :], xn_tm[:, :],
                                cc, deps=[hst, h_xnfm_last])
            hp = hap
            for k in range(8):
                hm = em.op("tensor", lambda E, k=k: E.transpose(
                    pt[:, :], xn_tm[:, k * 128:(k + 1) * 128], ident[:, :]),
                    deps=[hp, h_id])
                hc = em.op("vector", lambda E, k=k, c=c: E.tensor_copy(
                    xn_fm[k][:, c * 128:(c + 1) * 128], pt[:, :]), deps=[hm])
                hp = hc
            h_xnfm_last = hp
        h_p = []
        for m in range(2):
            def pmm(E, m=m):
                ins = None
                for k in range(8):
                    ins = E.matmul(ps[4][:, :], u1_sb[:, k * 256 + m * 128: k * 256 + m * 128 + 128],
                                   xn_fm[k][:, :], start=(k == 0), stop=(k == 7))
                return ins
            hmm = em.op("tensor", pmm, deps=[h_xnfm_last, h_u1] + ([h_p[-1]] if h_p else []))
            h_p.append(em.op("vector", lambda E, m=m: E.tensor_copy(
                p_fm[m][:, :], ps[4][:, :]), deps=[hmm]))
        h_hh = [None] * 32
        h_gelu_prev = None
        for ff in range(32):
            pb = ps[5] if ff % 2 == 0 else ps[6]
            def hmmf(E, ff=ff, pb=pb):
                ins = None
                for kk in range(2):
                    ins = E.matmul(pb[:, :], v1_sb[:, kk * 4096 + ff * 128: kk * 4096 + ff * 128 + 128],
                                   p_fm[kk][:, :], start=(kk == 0), stop=(kk == 1))
                return ins
            dd = [h_p[-1], h_v1]
            if ff >= 2:
                dd.append(h_hh[ff - 2])
            hmm = em.op("tensor", hmmf, deps=dd)
            h_hh[ff] = em.op("scalar", lambda E, ff=ff, pb=pb: E.activation(
                hh[:, ff * 512:(ff + 1) * 512], pb[:, :], AF.Gelu_apprx_tanh,
                bias=b1c[:, ff:ff + 1]), deps=[hmm, h_b1])
        h_q = []
        for m in range(2):
            def qmm(E, m=m):
                ins = None
                for k in range(32):
                    ins = E.matmul(ps[4][:, :], u2_sb[:, k * 256 + m * 128: k * 256 + m * 128 + 128],
                                   hh[:, k * 512:(k + 1) * 512], start=(k == 0), stop=(k == 31))
                return ins
            hmm = em.op("tensor", qmm, deps=[h_u2] + h_hh + ([h_q[-1]] if h_q else []))
            h_q.append(em.op("vector", lambda E, m=m: E.tensor_copy(
                q_fm[m][:, :], ps[4][:, :]), deps=[hmm]))
        h_qdone_prev = h_q[-1]
        h_oadd = []
        for c in range(4):
            hxb = em.op("vector", lambda E, c=c: E.tensor_add(
                x3_sb[c][:, :], x3_sb[c][:, :], wbc_b2[:, :]),
                deps=[h_x3[2 * c + 1]] + hb2 + [h_xnfm_last])
            for fh in range(2):
                def vmm(E, c=c, fh=fh):
                    ins = None
                    for kk in range(2):
                        ins = E.matmul(ps[c][:, :],
                                       q_fm[kk][:, c * 128:(c + 1) * 128],
                                       v2_sb[:, kk * 1024 + fh * 512: kk * 1024 + fh * 512 + 512],
                                       start=(kk == 0), stop=(kk == 1))
                    return ins
                dd = [h_q[-1], h_v2]
                if h_oadd:
                    dd.append(h_oadd[-1])
                hmm = em.op("tensor", vmm, deps=dd)
                h_oadd.append(em.op("vector", lambda E, c=c, fh=fh: E.tensor_add(
                    o_sb[c][:, fh * 512:(fh + 1) * 512], x3_sb[c][:, fh * 512:(fh + 1) * 512],
                    ps[c][:, :]), deps=[hmm, hxb]))
            h_st.append(em.dma("sync", x4_d[t0 + c * 128: t0 + (c + 1) * 128, :],
                               o_sb[c][:, :], deps=[h_oadd[-1]], chan="st"))
            h_last_use[c] = h_st[-1]
        for e in ENGS:
            em.barrier(e, h_st[-4:])
    for e in ENGS:
        em.barrier(e, h_st)
    em.emit()
    return nc


def run_l1b(p, x2_flat, ogn_flat):
    nc = build_l1b()
    T = 2048
    in_maps = []
    for c in range(8):
        in_maps.append({
            "x2": np.ascontiguousarray(x2_flat[c * T:(c + 1) * T]),
            "ogn": np.ascontiguousarray(ogn_flat[c * T:(c + 1) * T]),
            "wo": p["l1_Wo"], "u1": p["l1f_U1"], "v1": p["l1f_V1"],
            "b1c": np.ascontiguousarray(p["l1f_b1"].reshape(32, 128).T),
            "u2": p["l1f_U2"], "v2": p["l1f_V2"],
            "b2r": p["l1f_b2"].reshape(1, -1),
        })
    res = run_bass_kernel_spmd(nc, in_maps, core_ids=list(range(8)))
    return np.concatenate([r["x4"] for r in res.results], axis=0)



def build_l0ffn(T=2048):
    nc = bass.Bass("TRN2", target_bir_lowering=False)
    x1_d = nc.dram_tensor("x1", [T, 1024], F32, kind="ExternalInput").ap()
    ei_d = nc.dram_tensor("eiT", [2, T], I32, kind="ExternalInput").ap()
    ew_d = nc.dram_tensor("ewT", [2, T], F32, kind="ExternalInput").ap()
    w1a_d = nc.dram_tensor("w1a", [8, 1024, 64], F32, kind="ExternalInput").ap()
    w1b_d = nc.dram_tensor("w1b", [8, 64, 4096], F32, kind="ExternalInput").ap()
    w2a_d = nc.dram_tensor("w2a", [8, 4096, 64], F32, kind="ExternalInput").ap()
    w2b_d = nc.dram_tensor("w2b", [512, 1024], F32, kind="ExternalInput").ap()
    x2_d = nc.dram_tensor("x2", [T, 1024], F32, kind="ExternalOutput").ap()

    em = Em(nc)
    L = Lib(em)
    ident, h_id = L.ident128()
    ones_row, h_or = L.ones_row()
    ps = [nc.alloc_psum_tensor(f"ps{i}", [128, 512], F32) for i in range(7)]
    pt = nc.alloc_psum_tensor("pt", [128, 128], BF16)
    sb = L.sb

    w1a_sb = sb([128, 8 * 8 * 64], BF16, "w1a_sb")
    w2b_sb = sb([128, 4 * 1024], BF16, "w2b_sb")
    w1b_sb = [sb([64, 4096], BF16, f"w1b{i}") for i in range(2)]
    w2a_sb = [sb([128, 32 * 64], BF16, f"w2a{i}") for i in range(2)]
    ei_i = sb([1, 2 * T], I32, "ei_i")
    eiw = sb([1, 4 * T], F32, "eiw")
    crow = sb([1, T], F32, "crow")
    crow_dr = nc.dram_tensor("crow_dr", [8, T], F32)
    x1_sb = [sb([128, 1024], F32, f"x1s{i}") for i in range(4)]
    xn_tm = sb([128, 1024], BF16, "xn_tm0")
    xn_fm = [sb([128, 512], BF16, f"xnm{k}") for k in range(8)]
    h1_sb = sb([64, 512], BF16, "h1_sb")
    g2_sb = sb([128, 512], BF16, "g2_sb")
    cb_sb = sb([64, 512], F32, "cb_sb")
    h3c = [sb([128, 512], BF16, f"h3c{k}") for k in range(4)]
    sq2 = sb([128, 1024], BF16, "sq2m")
    cols = sb([128, 32], F32, "colsm")
    o_sb = [sb([128, 1024], F32, f"om{i}") for i in range(4)]

    h_w1a = em.dma("gpsimd", w1a_sb.ap().rearrange("p (e k r) -> p e k r", e=8, k=8),
                   w1a_d.rearrange("e (k p) r -> p e k r", p=128), chan="w1a")
    h_w2b = em.dma("gpsimd", w2b_sb.ap().rearrange("p (k m) -> p k m", k=4),
                   w2b_d.rearrange("(k p) m -> p k m", p=128), chan="w2b")
    h_eis = [em.dma("gpsimd", ei_i[:1, j * T:(j + 1) * T], ei_d[j:j + 1, :], chan="eiw")
             for j in range(2)]
    h_ews = [em.dma("gpsimd", eiw[:1, (2 + j) * T:(3 + j) * T], ew_d[j:j + 1, :], chan="eiw")
             for j in range(2)]
    h_eiw_all = h_eis + h_ews
    h_ew = h_eiw_all[-1]
    h_eif = em.op("vector", lambda E: E.tensor_copy(eiw[:1, 0:2 * T], ei_i[:1, :]),
                  deps=h_eiw_all)

    msk = sb([1, 2 * T], F32, "msk")
    h_cd = []
    hprev = None
    for e in range(8):
        h1 = em.op("vector", lambda E, e=e: E.tensor_scalar(
            msk[:1, :], eiw[:1, 0:2 * T], float(e), None, ALU.is_equal),
            deps=[h_eif, h_ew] + ([hprev] if hprev else []))
        h2 = em.op("vector", lambda E: E.tensor_mul(msk[:1, :], msk[:1, :], eiw[:1, 2 * T:4 * T]),
                   deps=[h1])
        h3 = em.op("vector", lambda E, e=e: E.tensor_add(
            crow[:1, :], msk[:1, 0:T], msk[:1, T:2 * T]), deps=[h2])
        hprev = em.dma("sync", crow_dr[e:e + 1, :], crow[:1, :], deps=[h3], chan="crowd")
        h_cd.append(hprev)
    crow_row = sb([1, 8 * T], BF16, "crow_row")
    h_crs = [em.dma("gpsimd", crow_row[:1, e * T:(e + 1) * T], crow_dr[e:e + 1, :],
                    deps=[h_cd[e]], chan="crowr") for e in range(8)]
    h_cr = h_crs[-1]

    h_st = []
    h_last = [None] * 4
    for tt in range(4):
        t0 = tt * 512
        h_x1 = []
        for c in range(4):
            dd = [h_last[c]] if h_last[c] else []
            h_x1.append(em.dma("sync", x1_sb[c][:, :], x1_d[t0 + c * 128: t0 + (c + 1) * 128, :],
                               deps=dd, chan=f"x1{c}"))
        hfm = None
        for c in range(4):
            cc = cols[:, c * 8:(c + 1) * 8]
            hst_ = emit_ln_stats(L, x1_sb[c][:, :], sq2[:, :], cc,
                                 deps=[h_x1[c], hfm])
            hap = emit_ln_apply(L, x1_sb[c][:, :], xn_tm[:, :], cc, deps=[hst_, hfm])
            hp = hap
            for k in range(8):
                hm = em.op("tensor", lambda E, k=k: E.transpose(
                    pt[:, :], xn_tm[:, k * 128:(k + 1) * 128], ident[:, :]),
                    deps=[hp, h_id])
                hc = em.op("vector", lambda E, k=k, c=c: E.tensor_copy(
                    xn_fm[k][:, c * 128:(c + 1) * 128], pt[:, :]), deps=[hm])
                hp = hc
            hfm = hp
        h_wl = [None, None]
        h_h3c = []
        for e in range(8):
            b = e % 2
            ddw = [h_wl[b]] if h_wl[b] else []
            hw1 = em.dma("gpsimd", w1b_sb[b][:, :], w1b_d[e], deps=ddw, chan=f"w1b{b}")
            hw2 = em.dma("gpsimd", w2a_sb[b].ap().rearrange("p (k r) -> p k r", k=32),
                         w2a_d[e].rearrange("(k p) r -> p k r", p=128),
                         deps=ddw, chan=f"w2a{b}")
            def h1mm(E, e=e):
                ins = None
                for k in range(8):
                    ins = E.matmul(ps[4][0:64, :], w1a_sb[:, (e * 8 + k) * 64:(e * 8 + k) * 64 + 64],
                                   xn_fm[k][:, :], start=(k == 0), stop=(k == 7))
                return ins
            dd = [hfm, h_w1a]
            if h_h3c:
                dd.append(h_h3c[-1])
            hmm1 = em.op("tensor", h1mm, deps=dd)
            hcp1 = em.op("vector", lambda E: E.tensor_copy(h1_sb[:, :], ps[4][0:64, :]),
                         deps=[hmm1])
            hcb = em.op("tensor", lambda E, e=e: E.matmul(
                ps[5][0:64, :], ones_row[:1, 0:64], crow_row[:1, e * T + t0: e * T + t0 + 512],
                start=True, stop=True),
                deps=[h_cr, h_or] + ([h_h3c[-1]] if h_h3c else []) + h_crs)
            hcbc = em.op("vector", lambda E: E.tensor_copy(cb_sb[:, :], ps[5][0:64, :]),
                         deps=[hcb])
            h_g_prev = None
            hmm3 = None
            for m in range(32):
                def h2mm(E, m=m, b=b):
                    return E.matmul(ps[6][:, :], w1b_sb[b][:, m * 128:(m + 1) * 128],
                                    h1_sb[:, :], start=True, stop=True)
                dd2 = [hcp1, hw1]
                if h_g_prev:
                    dd2.append(h_g_prev)
                hmm2 = em.op("tensor", h2mm, deps=dd2)
                hg = em.op("scalar", lambda E: E.activation(
                    g2_sb[:, :], ps[6][:, :], AF.Gelu), deps=[hmm2] + ([hmm3] if hmm3 else []))
                h_g_prev = hg
                def h3mm(E, m=m, b=b):
                    return E.matmul(ps[4][0:64, :], w2a_sb[b][:, m * 64:(m + 1) * 64],
                                    g2_sb[:, :], start=(m == 0), stop=(m == 31))
                hmm3 = em.op("tensor", h3mm, deps=[hg, hw2, hcp1])
            half = (e % 2) * 64
            qq = e // 2
            h_h3c.append(em.op("vector", lambda E, half=half, qq=qq: E.tensor_mul(
                h3c[qq][half:half + 64, :], ps[4][0:64, :], cb_sb[:, :]),
                deps=[hmm3, hcbc]))
            h_wl[b] = hmm3
        h_oadd = []
        for c in range(4):
            for fh in range(2):
                def ymm(E, c=c, fh=fh):
                    ins = None
                    for k in range(4):
                        ins = E.matmul(ps[c][:, :], h3c[k][:, c * 128:(c + 1) * 128],
                                       w2b_sb[:, k * 1024 + fh * 512: k * 1024 + fh * 512 + 512],
                                       start=(k == 0), stop=(k == 3))
                    return ins
                dd = h_h3c[-2:] + [h_w2b] + ([h_oadd[-1]] if h_oadd else [])
                hmm = em.op("tensor", ymm, deps=dd)
                h_oadd.append(em.op("vector", lambda E, c=c, fh=fh: E.tensor_add(
                    o_sb[c][:, fh * 512:(fh + 1) * 512],
                    x1_sb[c][:, fh * 512:(fh + 1) * 512], ps[c][:, :]),
                    deps=[hmm, h_x1[c]]))
            h_st.append(em.dma("sync", x2_d[t0 + c * 128: t0 + (c + 1) * 128, :],
                               o_sb[c][:, :], deps=[h_oadd[-1]], chan="st"))
            h_last[c] = h_st[-1]
        for e in ENGS:
            em.barrier(e, h_st[-4:])
    for e in ENGS:
        em.barrier(e, h_st)
    em.emit()
    return nc


def run_l0ffn(p, x1_flat):
    nc = build_l0ffn()
    T = 2048
    eiT = np.full((2, 16384), 8, np.int32)
    ewT = np.zeros((2, 16384), np.float32)
    eiT[:, :8192] = p["moe_ei"].T
    ewT[:, :8192] = p["moe_ew"].T
    in_maps = []
    for c in range(8):
        sl = slice(c * T, (c + 1) * T)
        in_maps.append({
            "x1": np.ascontiguousarray(x1_flat[sl]),
            "eiT": np.ascontiguousarray(eiT[:, sl]),
            "ewT": np.ascontiguousarray(ewT[:, sl]),
            "w1a": p["moe_w1A_T"], "w1b": p["moe_w1B"],
            "w2a": p["moe_w2A_T"],
            "w2b": np.ascontiguousarray(p["moe_w2B"].reshape(512, 1024)),
        })
    res = run_bass_kernel_spmd(nc, in_maps, core_ids=list(range(8)))
    return np.concatenate([r["x2"] for r in res.results], axis=0)



def build_l2b(T=2048):
    nc = bass.Bass("TRN2", target_bir_lowering=False)
    x4_d = nc.dram_tensor("x4", [T, 1024], F32, kind="ExternalInput").ap()
    osc_d = nc.dram_tensor("osc", [T, 2048], F32, kind="ExternalInput").ap()
    wout_d = nc.dram_tensor("wout", [2048, 1024], F32, kind="ExternalInput").ap()
    nw_d = nc.dram_tensor("nwr", [1, 2048], F32, kind="ExternalInput").ap()
    x5_d = nc.dram_tensor("x5", [T, 1024], F32, kind="ExternalOutput").ap()
    xmax_d = nc.dram_tensor("xmax", [1, 1], F32, kind="ExternalOutput").ap()

    em = Em(nc)
    L = Lib(em)
    ident, h_id = L.ident128()
    ones_row, h_or = L.ones_row()
    ps = [nc.alloc_psum_tensor(f"ps{i}", [128, 512], F32) for i in range(6)]
    pt = nc.alloc_psum_tensor("pt", [128, 128], BF16)
    ptf = nc.alloc_psum_tensor("ptf", [128, 128], F32)
    sb = L.sb

    wout_sb = sb([128, 16 * 1024], BF16, "wout_sb")
    wbc_nw = sb([128, 2048], BF16, "wbc_nw")
    rowbuf = sb([1, 2048], BF16, "rb2")
    osc_sb = sb([128, 2048], F32, "osc_sb")
    nrm_tm = sb([128, 2048], BF16, "nrm2")
    nrm_fm = [sb([128, 512], BF16, f"n2f{k}") for k in range(16)]
    x5_sb = [sb([128, 1024], F32, f"x5s{i}") for i in range(4)]
    lnbuf = sb([128, 1024], F32, "lnbuf")
    sq2 = sb([128, 2048], BF16, "sq22")
    cols = sb([128, 32], F32, "cols2")
    mx = sb([128, 4], F32, "mx")

    h_w = em.dma("gpsimd", wout_sb.ap().rearrange("p (k m) -> p k m", k=16),
                 wout_d.rearrange("(k p) m -> p k m", p=128), chan="wout")
    h_nw = em.dma("gpsimd", rowbuf[:1, :], nw_d[:, :], chan="nw")
    hb = []
    dep = [h_nw, h_or]
    for j in range(0, 2048, 512):
        hm = em.op("tensor", lambda E, j=j: E.matmul(
            ps[5][:, :], ones_row[:1, :], rowbuf[:1, j:j + 512],
            start=True, stop=True), deps=dep)
        hc = em.op("vector", lambda E, j=j: E.tensor_copy(
            wbc_nw[:, j:j + 512], ps[5][:, :]), deps=[hm])
        hb.append(hc)
        dep = dep + [hc]

    h_st = []
    h_maxinit = em.op("vector", lambda E: E.memset(mx[:, 0:1], 0.0))
    hmax_prev = h_maxinit
    h_last = None
    for tt in range(4):
        t0 = tt * 512
        h_nfm_all = None
        for c in range(4):
            dd = [h_last] if h_last else []
            h_osc = em.dma("sync", osc_sb[:, :], osc_d[t0 + c * 128: t0 + c * 128 + 128, :],
                           deps=dd + [hmax_prev], chan="osc")
            cc = cols[:, c * 8:(c + 1) * 8]
            hsq = em.op("scalar", lambda E: E.activation(sq2[:, :], osc_sb[:, :], AF.Square),
                        deps=[h_osc, hmax_prev])
            hss = em.op("vector", lambda E, cc=cc: E.reduce_sum(
                cc[:, 0:1], sq2[:, :], axis=mybir.AxisListType.X), deps=[hsq])
            hm1 = em.op("vector", lambda E, cc=cc: E.tensor_scalar(
                cc[:, 1:2], cc[:, 0:1], 1.0 / 2048.0, 1.1920929e-07, ALU.mult, ALU.add),
                deps=[hss])
            hs1 = em.op("scalar", lambda E, cc=cc: E.activation(cc[:, 2:3], cc[:, 1:2], AF.Sqrt),
                        deps=[hm1])
            hr1 = em.op("vector", lambda E, cc=cc: E.reciprocal(cc[:, 3:4], cc[:, 2:3]),
                        deps=[hs1])
            hap = em.op("vector", lambda E, cc=cc: E.scalar_tensor_tensor(
                nrm_tm[:, :], osc_sb[:, :], cc[:, 3:4], wbc_nw[:, :], ALU.mult, ALU.mult),
                deps=[hr1] + hb)
            hp = hap
            for k in range(16):
                hm = em.op("tensor", lambda E, k=k: E.transpose(
                    pt[:, :], nrm_tm[:, k * 128:(k + 1) * 128], ident[:, :]), deps=[hp, h_id])
                hc = em.op("vector", lambda E, k=k, c=c: E.tensor_copy(
                    nrm_fm[k][:, c * 128:(c + 1) * 128], pt[:, :]), deps=[hm])
                hp = hc
            h_nfm_all = hp
            hmax_prev = hp
        h_x4 = []
        for c in range(4):
            dd = ([h_last] if h_last else []) + [hmax_prev]
            h_x4.append(em.dma("sync", x5_sb[c][:, :], x4_d[t0 + c * 128: t0 + c * 128 + 128, :],
                               deps=dd, chan=f"x4{c}"))
        hmax_c = None
        for c in range(4):
            for fh in range(2):
                def mmfn(E, c=c, fh=fh):
                    ins = None
                    for k in range(16):
                        ins = E.matmul(ps[c][:, :], nrm_fm[k][:, c * 128:(c + 1) * 128],
                                       wout_sb[:, k * 1024 + fh * 512: k * 1024 + fh * 512 + 512],
                                       start=(k == 0), stop=(k == 15))
                    return ins
                dd = [h_nfm_all, h_w]
                if fh == 1:
                    dd.append(h_add)
                hmm = em.op("tensor", mmfn, deps=dd)
                h_add = em.op("vector", lambda E, c=c, fh=fh: E.tensor_add(
                    x5_sb[c][:, fh * 512:(fh + 1) * 512],
                    x5_sb[c][:, fh * 512:(fh + 1) * 512], ps[c][:, :]),
                    deps=[hmm, h_x4[c]])
            h_st.append(em.dma("sync", x5_d[t0 + c * 128: t0 + (c + 1) * 128, :],
                               x5_sb[c][:, :], deps=[h_add], chan="st"))
            cc = cols[:, c * 8:(c + 1) * 8]
            hst_ = emit_ln_stats(L, x5_sb[c][:, :], sq2[:, :1024], cc,
                                 deps=[h_add, hmax_c if hmax_c else hmax_prev])
            hap2 = emit_ln_apply(L, x5_sb[c][:, :], lnbuf[:, :], cc,
                                 deps=[hst_, hmax_c if hmax_c else hmax_prev])
            hm2 = em.op("vector", lambda E: E.reduce_max(
                mx[:, 1:2], lnbuf[:, :], axis=mybir.AxisListType.X,
                apply_absolute_value=True), deps=[hap2])
            hmax_c = em.op("vector", lambda E: E.tensor_max(mx[:, 0:1], mx[:, 0:1], mx[:, 1:2]),
                           deps=[hm2, hmax_prev if tt == 0 and c == 0 else None])
        hmax_prev = hmax_c
        h_last = h_st[-1]
        for e in ENGS:
            em.barrier(e, h_st[-4:])
    mxrow = sb([1, 128], F32, "mxrow")
    identf, h_idf = L.ident128f()
    hT = em.op("tensor", lambda E: E.transpose(ptf[0:1, :], mx[:, 0:1], identf[:, :]),
               deps=[hmax_prev, h_idf])
    hcp = em.op("vector", lambda E: E.tensor_copy(mxrow[:1, :], ptf[0:1, 0:128]), deps=[hT])
    hred = em.op("vector", lambda E: E.reduce_max(mx[0:1, 2:3], mxrow[:1, :],
                                                  axis=mybir.AxisListType.X), deps=[hcp])
    h_st.append(em.dma("sync", xmax_d[:, :], mx[0:1, 2:3], deps=[hred], chan="st"))
    for e in ENGS:
        em.barrier(e, h_st)
    em.emit()
    return nc



def build_l2f1(T=2048):
    nc = bass.Bass("TRN2", target_bir_lowering=False)
    x5_d = nc.dram_tensor("x5", [T, 1024], F32, kind="ExternalInput").ap()
    xmax8_d = nc.dram_tensor("xmax8", [1, 8], F32, kind="ExternalInput").ap()
    clip_d = nc.dram_tensor("clipx", [1, 1], F32, kind="ExternalInput").ap()
    gw_d = nc.dram_tensor("gwq", [1024, 2816], F32, kind="ExternalInput").ap()
    uw_d = nc.dram_tensor("uwq", [1024, 2816], F32, kind="ExternalInput").ap()
    h_d = nc.dram_tensor("hout", [T, 2816], F32, kind="ExternalOutput").ap()
    hmax_d = nc.dram_tensor("hmax", [1, 1], F32, kind="ExternalOutput").ap()

    em = Em(nc)
    L = Lib(em)
    ident, h_id = L.ident128()
    identf, h_idf = L.ident128f()
    ones_row, h_or = L.ones_rowf()
    ps = [nc.alloc_psum_tensor(f"ps{i}", [128, 512], F32) for i in range(6)]
    pt = nc.alloc_psum_tensor("pt", [128, 128], BF16)
    ptf = nc.alloc_psum_tensor("ptf", [128, 128], F32)
    sb = L.sb
    FF = 2816
    NK = 8
    NFC = 22

    wq_g = sb([128, NK * FF], BF16, "wq_g")
    wq_u = sb([128, NK * FF], BF16, "wq_u")
    wbuf = sb([128, FF], F32, "wbuf")
    wmax = sb([128, FF], F32, "wmax")
    sc_g = sb([128, FF], F32, "sc_g")
    sc_u = sb([128, FF], F32, "sc_u")
    rbc = sb([128, FF], F32, "rbc")
    wsc_cols = sb([128, NFC * 2], F32, "wsc_cols")
    srow = sb([22, 128], F32, "srow")
    rowsc = nc.dram_tensor("rowsc", [22, 128], F32)
    rowbuf = sb([1, FF], F32, "rb3")
    scal = sb([128, 8], F32, "scal")
    sc1 = sb([1, 8], F32, "sc1")
    x5_sb = [sb([128, 1024], F32, f"x5f{i}") for i in range(4)]
    xq_tm = sb([128, 1024], BF16, "xq_tm")
    xq_fm = [sb([128, 512], BF16, f"xqf{k}") for k in range(8)]
    sg = sb([128, 512], BF16, "sgf")
    hbuf = sb([128, FF], F32, "hbuf")
    sq2 = sb([128, 1024], BF16, "sq2f")
    cols = sb([128, 32], F32, "colsf")
    mx = sb([128, 4], F32, "mxf")

    h_x8 = em.dma("sync", sc1[:1, :], xmax8_d[:, :], chan="x8")
    h_cl = em.dma("sync", scal[0:1, 4:5], clip_d[:, :], chan="cl")
    hg1 = em.op("vector", lambda E: E.reduce_max(scal[0:1, 5:6], sc1[:1, :],
                                                 axis=mybir.AxisListType.X), deps=[h_x8])
    hg2 = em.op("scalar", lambda E: E.activation(scal[0:1, 6:7], scal[0:1, 4:5], AF.Sigmoid),
                deps=[h_cl])
    hg3 = em.op("vector", lambda E: E.tensor_mul(scal[0:1, 6:7], scal[0:1, 6:7], scal[0:1, 5:6]),
                deps=[hg1, hg2])
    hg4 = em.op("vector", lambda E: E.tensor_scalar(scal[0:1, 6:7], scal[0:1, 6:7],
                                                    1.0 / 127.0, None, ALU.mult), deps=[hg3])
    hg5 = em.op("vector", lambda E: E.reciprocal(scal[0:1, 7:8], scal[0:1, 6:7]), deps=[hg4])
    hb1 = em.op("tensor", lambda E: E.matmul(ptf[:, 0:2], ones_row[:1, :],
                                             scal[0:1, 6:8], start=True, stop=True),
                deps=[hg5, h_or])
    hbc = em.op("vector", lambda E: E.tensor_copy(scal[:, 2:4], ptf[:, 0:2]), deps=[hb1])

    def quant_weight(w_d, wq_sb, sc_sb, cidx, tag, deps0):
        hs = []
        h_ld = None
        for k in range(NK):
            h_ld = em.dma("gpsimd", wbuf[:, :], w_d[k * 128:(k + 1) * 128, :],
                          deps=[h_ld] if h_ld else deps0, chan=f"wq_{tag}")
            hab = em.op("scalar", lambda E: E.activation(wbuf[:, :], wbuf[:, :], AF.Abs),
                        deps=[h_ld] + ([h_ld2] if k > 0 else []))
            if k == 0:
                h_ld2 = em.op("vector", lambda E: E.tensor_copy(wmax[:, :], wbuf[:, :]),
                              deps=[hab])
            else:
                h_ld2 = em.op("vector", lambda E: E.tensor_max(
                    wmax[:, :], wmax[:, :], wbuf[:, :]), deps=[hab, h_ld2])
            h_ld = h_ld2
        habs = h_ld2
        hcols = []
        hp = habs
        for f in range(NFC):
            hm = em.op("tensor", lambda E, f=f: E.transpose(
                ptf[:, :], wmax[:, f * 128:(f + 1) * 128], identf[:, :]), deps=[hp, h_idf])
            hr = em.op("vector", lambda E, f=f: E.reduce_max(
                wsc_cols[:, cidx * NFC + f: cidx * NFC + f + 1], ptf[:, :],
                axis=mybir.AxisListType.X), deps=[hm])
            hcols.append(hr)
            hp = hr
        hsc = em.op("vector", lambda E: E.tensor_scalar(
            wsc_cols[:, cidx * NFC:(cidx + 1) * NFC], wsc_cols[:, cidx * NFC:(cidx + 1) * NFC],
            1.0 / 127.0, None, ALU.mult), deps=hcols)
        hT = em.op("tensor", lambda E: E.transpose(
            ptf[0:22, :], wsc_cols[:, cidx * NFC:(cidx + 1) * NFC], identf[:, :]),
            deps=[hsc, h_idf])
        hcp = em.op("vector", lambda E: E.tensor_copy(srow[:, :], ptf[0:22, 0:128]), deps=[hT])
        hdr = em.dma("sync", rowsc[:, :], srow[:, :], deps=[hcp], chan=f"rsc_{tag}")
        hrow = em.dma("sync", rowbuf[:1, :], rowsc.ap().rearrange("a b -> (a b)"),
                      deps=[hdr], chan=f"rrow_{tag}")
        hsb = []
        dep = [hrow, h_or, hbc]
        for j in range(0, FF, 512):
            w = min(512, FF - j)
            hm = em.op("tensor", lambda E, j=j, w=w: E.matmul(
                ps[5][:, :w], ones_row[:1, :], rowbuf[:1, j:j + w], start=True, stop=True),
                deps=dep)
            h1 = em.op("vector", lambda E, j=j, w=w: E.tensor_scalar(
                sc_sb[:, j:j + w], ps[5][:, :w], scal[:, 2:3], None, ALU.mult), deps=[hm])
            h2 = em.op("vector", lambda E, j=j, w=w: E.reciprocal(
                rbc[:, j:j + w], ps[5][:, :w]), deps=[hm, h1])
            hsb.append(h2)
            dep = dep + [h2]
        h_ld = hsb[-1]
        hq = []
        for k in range(NK):
            h_ld = em.dma("gpsimd", wbuf[:, :], w_d[k * 128:(k + 1) * 128, :],
                          deps=[h_ld], chan=f"wq_{tag}")
            ht1 = em.op("vector", lambda E: E.tensor_mul(wbuf[:, :], wbuf[:, :], rbc[:, :]),
                        deps=[h_ld] + hsb)
            ht2 = em.op("vector", lambda E, k=k: E.tensor_scalar(
                wq_sb[:, k * FF:(k + 1) * FF], wbuf[:, :], RND_C, RND_C,
                ALU.add, ALU.subtract), deps=[ht1])
            hq.append(ht2)
            h_ld = ht2
        return hq

    hq_g = quant_weight(gw_d, wq_g, sc_g, 0, "g", [])
    hq_u = quant_weight(uw_d, wq_u, sc_u, 1, "u", hq_g)

    lnb = sb([128, 1024], F32, "lnb")
    h_st = []
    h_last = None
    h_maxinit = em.op("vector", lambda E: E.memset(mx[:, 0:1], 0.0))
    hmax_prev = h_maxinit
    FSL = [(j, min(512, FF - j)) for j in range(0, FF, 512)]
    for tt in range(4):
        t0 = tt * 512
        h_x5 = []
        for c in range(4):
            dd = [h_last] if h_last else []
            h_x5.append(em.dma("sync", x5_sb[c][:, :], x5_d[t0 + c * 128: t0 + c * 128 + 128, :],
                               deps=dd, chan=f"x5{c}"))
        hfm = None
        for c in range(4):
            cc = cols[:, c * 8:(c + 1) * 8]
            hst_ = emit_ln_stats(L, x5_sb[c][:, :], sq2[:, :], cc, deps=[h_x5[c], hfm])
            hap = emit_ln_apply(L, x5_sb[c][:, :], lnb[:, :], cc, deps=[hst_, hfm])
            hq1 = em.op("scalar", lambda E: E.activation(
                lnb[:, :], lnb[:, :], AF.Copy, scale=scal[:, 3:4]), deps=[hap, hbc])
            hq2 = em.op("vector", lambda E: E.tensor_scalar(
                lnb[:, :], lnb[:, :], RND_C, RND_C, ALU.add, ALU.subtract), deps=[hq1])
            hq3 = em.op("vector", lambda E: E.tensor_scalar(
                xq_tm[:, :], lnb[:, :], 127.0, -128.0, ALU.min, ALU.max), deps=[hq2])
            hp = hq3
            for k in range(8):
                hm = em.op("tensor", lambda E, k=k: E.transpose(
                    pt[:, :], xq_tm[:, k * 128:(k + 1) * 128], ident[:, :]), deps=[hp, h_id])
                hc = em.op("vector", lambda E, k=k, c=c: E.tensor_copy(
                    xq_fm[k][:, c * 128:(c + 1) * 128], pt[:, :]), deps=[hm])
                hp = hc
            hfm = hp
        for c in range(4):
            h_hs = []
            for (j, w) in FSL:
                def gmm(E, c=c, j=j, w=w):
                    ins = None
                    for k in range(8):
                        ins = E.matmul(ps[0][:, :w], xq_fm[k][:, c * 128:(c + 1) * 128],
                                       wq_g[:, k * FF + j: k * FF + j + w],
                                       start=(k == 0), stop=(k == 7))
                    return ins
                def umm(E, c=c, j=j, w=w):
                    ins = None
                    for k in range(8):
                        ins = E.matmul(ps[1][:, :w], xq_fm[k][:, c * 128:(c + 1) * 128],
                                       wq_u[:, k * FF + j: k * FF + j + w],
                                       start=(k == 0), stop=(k == 7))
                    return ins
                ddg = [hfm] + hq_g + ([h_sil] if h_st or h_hs else [])
                ddu = [hfm] + hq_u + ([h_hmul] if h_st or h_hs else [])
                hgm = em.op("tensor", gmm, deps=ddg)
                hum = em.op("tensor", umm, deps=ddu)
                hdq = em.op("vector", lambda E, j=j, w=w: E.tensor_mul(
                    sg[:, :w], ps[0][:, :w], sc_g[:, j:j + w]),
                    deps=[hgm] + ([h_hmul] if h_st or h_hs else []))
                h_sil = em.op("scalar", lambda E, j=j, w=w: E.activation(
                    sg[:, :w], sg[:, :w], AF.Silu), deps=[hdq])
                hdqu = em.op("vector", lambda E, j=j, w=w: E.tensor_mul(
                    hbuf[:, j:j + w], ps[1][:, :w], sc_u[:, j:j + w]), deps=[hum, h_sil])
                h_hmul = em.op("vector", lambda E, j=j, w=w: E.tensor_mul(
                    hbuf[:, j:j + w], hbuf[:, j:j + w], sg[:, :w]), deps=[hdqu])
                h_hs.append(h_hmul)
            hm2 = em.op("vector", lambda E: E.reduce_max(
                mx[:, 1:2], hbuf[:, :], axis=mybir.AxisListType.X,
                apply_absolute_value=True), deps=h_hs)
            hmax_prev = em.op("vector", lambda E: E.tensor_max(
                mx[:, 0:1], mx[:, 0:1], mx[:, 1:2]), deps=[hm2, hmax_prev])
            h_st.append(em.dma("sync", h_d[t0 + c * 128: t0 + c * 128 + 128, :],
                               hbuf[:, :], deps=h_hs, chan="st"))
            h_last = h_st[-1]
            em.barrier("vector", [h_last])
        for e in ENGS:
            em.barrier(e, h_st[-4:])
    mxrow = sb([1, 128], F32, "mxrow1")
    hT = em.op("tensor", lambda E: E.transpose(ptf[0:1, :], mx[:, 0:1], identf[:, :]),
               deps=[hmax_prev, h_idf])
    hcp2 = em.op("vector", lambda E: E.tensor_copy(mxrow[:1, :], ptf[0:1, 0:128]), deps=[hT])
    hred = em.op("vector", lambda E: E.reduce_max(mx[0:1, 2:3], mxrow[:1, :],
                                                  axis=mybir.AxisListType.X), deps=[hcp2])
    h_st.append(em.dma("sync", hmax_d[:, :], mx[0:1, 2:3], deps=[hred], chan="st"))
    for e in ENGS:
        em.barrier(e, h_st)
    em.emit()
    return nc



def build_l2f2(T=2048):
    nc = bass.Bass("TRN2", target_bir_lowering=False)
    x5_d = nc.dram_tensor("x5", [T, 1024], F32, kind="ExternalInput").ap()
    h_d = nc.dram_tensor("hin", [T, 2816], F32, kind="ExternalInput").ap()
    hmax8_d = nc.dram_tensor("hmax8", [1, 8], F32, kind="ExternalInput").ap()
    clip_d = nc.dram_tensor("clipd", [1, 1], F32, kind="ExternalInput").ap()
    dw_d = nc.dram_tensor("dwq", [2816, 1024], F32, kind="ExternalInput").ap()
    x6_d = nc.dram_tensor("x6", [T, 1024], F32, kind="ExternalOutput").ap()

    em = Em(nc)
    L = Lib(em)
    ident, h_id = L.ident128()
    identf, h_idf = L.ident128f()
    ones_row, h_or = L.ones_rowf()
    ps = [nc.alloc_psum_tensor(f"ps{i}", [128, 512], F32) for i in range(6)]
    pt = nc.alloc_psum_tensor("pt", [128, 128], BF16)
    ptf = nc.alloc_psum_tensor("ptf", [128, 128], F32)
    sb = L.sb
    FI, FO, NK, NFC = 2816, 1024, 22, 8

    wq_dn = sb([128, NK * FO], BF16, "wq_dn")
    wbuf = sb([128, FO], F32, "wbufd")
    wmax = sb([128, FO], F32, "wmaxd")
    sc_d = sb([128, FO], F32, "sc_d")
    rbc = sb([128, FO], F32, "rbcd")
    wsc_cols = sb([128, NFC], F32, "wscd")
    srow = sb([8, 128], F32, "srowd")
    rowsc = nc.dram_tensor("rowscd", [8, 128], F32)
    rowbuf = sb([1, FO], F32, "rb4")
    scal = sb([128, 8], F32, "scald")
    sc1 = sb([1, 8], F32, "sc1d")
    x5_sb = [sb([128, 1024], F32, f"x5g{i}") for i in range(4)]
    hq_tm = sb([128, 2816], BF16, "hq_tm")
    hbuf = sb([128, 2816], F32, "hbufd")
    hq_fm = [sb([128, 512], BF16, f"hqf{k}") for k in range(22)]
    o_sb = sb([128, 1024], F32, "od")

    h_x8 = em.dma("sync", sc1[:1, :], hmax8_d[:, :], chan="x8")
    h_cl = em.dma("sync", scal[0:1, 4:5], clip_d[:, :], chan="cl")
    hg1 = em.op("vector", lambda E: E.reduce_max(scal[0:1, 5:6], sc1[:1, :],
                                                 axis=mybir.AxisListType.X), deps=[h_x8])
    hg2 = em.op("scalar", lambda E: E.activation(scal[0:1, 6:7], scal[0:1, 4:5], AF.Sigmoid),
                deps=[h_cl])
    hg3 = em.op("vector", lambda E: E.tensor_mul(scal[0:1, 6:7], scal[0:1, 6:7], scal[0:1, 5:6]),
                deps=[hg1, hg2])
    hg4 = em.op("vector", lambda E: E.tensor_scalar(scal[0:1, 6:7], scal[0:1, 6:7],
                                                    1.0 / 127.0, None, ALU.mult), deps=[hg3])
    hg5 = em.op("vector", lambda E: E.reciprocal(scal[0:1, 7:8], scal[0:1, 6:7]), deps=[hg4])
    hb1 = em.op("tensor", lambda E: E.matmul(ptf[:, 0:2], ones_row[:1, :],
                                             scal[0:1, 6:8], start=True, stop=True),
                deps=[hg5, h_or])
    hbc = em.op("vector", lambda E: E.tensor_copy(scal[:, 2:4], ptf[:, 0:2]), deps=[hb1])

    h_ld = None
    h_ld0 = None
    for k in range(NK):
        h_ld = em.dma("gpsimd", wbuf[:, :], dw_d[k * 128:(k + 1) * 128, :],
                      deps=[h_ld] if h_ld else [], chan="wq_d")
        hab = em.op("scalar", lambda E: E.activation(wbuf[:, :], wbuf[:, :], AF.Abs),
                    deps=[h_ld] + ([h_ld0] if h_ld0 else []))
        if k == 0:
            h_ld = em.op("vector", lambda E: E.tensor_copy(wmax[:, :], wbuf[:, :]), deps=[hab])
        else:
            h_ld = em.op("vector", lambda E: E.tensor_max(
                wmax[:, :], wmax[:, :], wbuf[:, :]), deps=[hab, h_ld0])
        h_ld0 = h_ld
    habs = h_ld0
    hcols = []
    hp = habs
    for f in range(NFC):
        hm = em.op("tensor", lambda E, f=f: E.transpose(
            ptf[:, :], wmax[:, f * 128:(f + 1) * 128], identf[:, :]), deps=[hp, h_idf])
        hr = em.op("vector", lambda E, f=f: E.reduce_max(
            wsc_cols[:, f:f + 1], ptf[:, :], axis=mybir.AxisListType.X), deps=[hm])
        hcols.append(hr)
        hp = hr
    hsc = em.op("vector", lambda E: E.tensor_scalar(
        wsc_cols[:, :], wsc_cols[:, :], 1.0 / 127.0, None, ALU.mult), deps=hcols)
    hT = em.op("tensor", lambda E: E.transpose(ptf[0:8, :], wsc_cols[:, :], identf[:, :]),
               deps=[hsc, h_idf])
    hcp = em.op("vector", lambda E: E.tensor_copy(srow[:, :], ptf[0:8, 0:128]), deps=[hT])
    hdr = em.dma("sync", rowsc[:, :], srow[:, :], deps=[hcp], chan="rscd")
    hrow = em.dma("sync", rowbuf[:1, :], rowsc.ap().rearrange("a b -> (a b)"),
                  deps=[hdr], chan="rrowd")
    hsb = []
    dep = [hrow, h_or, hbc]
    for j in range(0, FO, 512):
        hm = em.op("tensor", lambda E, j=j: E.matmul(
            ps[5][:, :], ones_row[:1, :], rowbuf[:1, j:j + 512], start=True, stop=True),
            deps=dep)
        h1 = em.op("vector", lambda E, j=j: E.tensor_scalar(
            sc_d[:, j:j + 512], ps[5][:, :], scal[:, 2:3], None, ALU.mult), deps=[hm])
        h2 = em.op("vector", lambda E, j=j: E.reciprocal(rbc[:, j:j + 512], ps[5][:, :]),
                   deps=[hm, h1])
        hsb.append(h2)
        dep = dep + [h2]
    h_ld = hsb[-1]
    hq_w = []
    for k in range(NK):
        h_ld = em.dma("gpsimd", wbuf[:, :], dw_d[k * 128:(k + 1) * 128, :],
                      deps=[h_ld], chan="wq_d")
        ht1 = em.op("vector", lambda E: E.tensor_mul(wbuf[:, :], wbuf[:, :], rbc[:, :]),
                    deps=[h_ld] + hsb)
        ht2 = em.op("vector", lambda E, k=k: E.tensor_scalar(
            wq_dn[:, k * FO:(k + 1) * FO], wbuf[:, :], RND_C, RND_C,
            ALU.add, ALU.subtract), deps=[ht1])
        hq_w.append(ht2)
        h_ld = ht2

    h_st = []
    h_last = None
    h_oadd = None
    for tt in range(4):
        t0 = tt * 512
        for c in range(4):
            r0 = t0 + c * 128
            dd = [h_last] if h_last else []
            h_x5 = em.dma("sync", x5_sb[c][:, :], x5_d[r0:r0 + 128, :], deps=dd, chan=f"x5{c}")
            h_h = em.dma("sync", hbuf[:, :], h_d[r0:r0 + 128, :],
                         deps=dd + ([h_hq3] if tt + c > 0 else []), chan="hh")
            h_hq1 = em.op("scalar", lambda E: E.activation(
                hbuf[:, :], hbuf[:, :], AF.Copy, scale=scal[:, 3:4]), deps=[h_h, hbc])
            h_hq2 = em.op("vector", lambda E: E.tensor_scalar(
                hbuf[:, :], hbuf[:, :], RND_C, RND_C, ALU.add, ALU.subtract), deps=[h_hq1])
            h_hq3 = em.op("vector", lambda E: E.tensor_scalar(
                hq_tm[:, :], hbuf[:, :], 127.0, -128.0, ALU.min, ALU.max), deps=[h_hq2])
            hp = h_hq3
            for k in range(22):
                hm = em.op("tensor", lambda E, k=k: E.transpose(
                    pt[:, :], hq_tm[:, k * 128:(k + 1) * 128], ident[:, :]), deps=[hp, h_id])
                hc = em.op("vector", lambda E, k=k: E.tensor_copy(
                    hq_fm[k][:, 0:128], pt[:, :]), deps=[hm])
                hp = hc
            for fh in range(2):
                def dmm(E, fh=fh):
                    ins = None
                    for k in range(22):
                        ins = E.matmul(ps[fh][:, :], hq_fm[k][:, 0:128],
                                       wq_dn[:, k * FO + fh * 512: k * FO + fh * 512 + 512],
                                       start=(k == 0), stop=(k == 21))
                    return ins
                dd2 = [hp] + hq_w + ([h_oadd] if h_oadd else [])
                hmm = em.op("tensor", dmm, deps=dd2)
                hdq = em.op("vector", lambda E, fh=fh: E.tensor_mul(
                    o_sb[:, fh * 512:(fh + 1) * 512], ps[fh][:, :],
                    sc_d[:, fh * 512:(fh + 1) * 512]), deps=[hmm])
                h_oadd = em.op("vector", lambda E, c=c, fh=fh: E.tensor_add(
                    o_sb[:, fh * 512:(fh + 1) * 512], o_sb[:, fh * 512:(fh + 1) * 512],
                    x5_sb[c][:, fh * 512:(fh + 1) * 512]), deps=[hdq, h_x5])
            h_st.append(em.dma("sync", x6_d[r0:r0 + 128, :], o_sb[:, :],
                               deps=[h_oadd], chan="st"))
            h_last = h_st[-1]
            em.barrier("vector", [h_last])
            em.barrier("scalar", [h_last])
    for e in ENGS:
        em.barrier(e, h_st)
    em.emit()
    return nc



def build_l3b(T=2048):
    nc = bass.Bass("TRN2", target_bir_lowering=False)
    x6_d = nc.dram_tensor("x6", [T, 1024], F32, kind="ExternalInput").ap()
    o_d = nc.dram_tensor("orow", [T, 1024], BF16, kind="ExternalInput").ap()
    po_d = nc.dram_tensor("po", [1024, 1024], F32, kind="ExternalInput").ap()
    wq_d = nc.dram_tensor("wqi", [1024, 1024], I8, kind="ExternalInput").ap()
    wsc_d = nc.dram_tensor("wscr", [1, 1024], F32, kind="ExternalInput").ap()
    bias_d = nc.dram_tensor("biasr", [1, 1024], F32, kind="ExternalInput").ap()
    x8_d = nc.dram_tensor("x8", [T, 1024], F32, kind="ExternalOutput").ap()

    em = Em(nc)
    L = Lib(em)
    ident, h_id = L.ident128()
    ones_row, h_or = L.ones_rowf()
    ps = [nc.alloc_psum_tensor(f"ps{i}", [128, 512], F32) for i in range(6)]
    pt = nc.alloc_psum_tensor("pt", [128, 128], BF16)
    sb = L.sb

    po_sb = sb([128, 8 * 1024], BF16, "po_sb")
    wq_i8 = sb([128, 8 * 1024], I8, "wq_i8")
    wq_bf = sb([128, 8 * 1024], BF16, "wq_bf")
    wbc_sc = sb([128, 1024], F32, "wbc_sc")
    wbc_bi = sb([128, 1024], F32, "wbc_bi")
    rowbuf = sb([1, 1024], F32, "rb5")
    o_rows = [sb([128, 1024], BF16, f"or{i}") for i in range(4)]
    o_fm = [sb([128, 512], BF16, f"of{k}") for k in range(8)]
    x7_sb = [sb([128, 1024], F32, f"x7s{i}") for i in range(4)]
    lnb = sb([128, 1024], F32, "lnb3")
    xq_tm = sb([128, 1024], BF16, "xq3")
    xq_fm = [sb([128, 512], BF16, f"xq3f{k}") for k in range(8)]
    amx = sb([128, 8], F32, "amx")
    out_sb = [sb([128, 1024], F32, f"o3{i}") for i in range(4)]
    sq2 = sb([128, 1024], BF16, "sq23")
    cols = sb([128, 32], F32, "cols3")

    h_po = em.dma("gpsimd", po_sb.ap().rearrange("p (k m) -> p k m", k=8),
                  po_d.rearrange("(k p) m -> p k m", p=128), chan="po")
    h_wq = em.dma("sync", wq_i8.ap().rearrange("p (k m) -> p k m", k=8),
                  wq_d.rearrange("(k p) m -> p k m", p=128), chan="wq")
    h_wqb = em.op("vector", lambda E: E.tensor_copy(wq_bf[:, :], wq_i8[:, :]), deps=[h_wq])
    h_ws = em.dma("sync", rowbuf[:1, :], wsc_d[:, :], chan="wsr")
    hbs = []
    dep = [h_ws, h_or]
    for j in range(0, 1024, 512):
        hm = em.op("tensor", lambda E, j=j: E.matmul(
            ps[5][:, :], ones_row[:1, :], rowbuf[:1, j:j + 512], start=True, stop=True),
            deps=dep)
        hc = em.op("vector", lambda E, j=j: E.tensor_copy(wbc_sc[:, j:j + 512], ps[5][:, :]),
                   deps=[hm])
        hbs.append(hc)
        dep = dep + [hc]
    h_bi = em.dma("sync", rowbuf[:1, :], bias_d[:, :], deps=hbs, chan="wsr")
    dep = [h_bi]
    for j in range(0, 1024, 512):
        hm = em.op("tensor", lambda E, j=j: E.matmul(
            ps[5][:, :], ones_row[:1, :], rowbuf[:1, j:j + 512], start=True, stop=True),
            deps=dep)
        hc = em.op("vector", lambda E, j=j: E.tensor_copy(wbc_bi[:, j:j + 512], ps[5][:, :]),
                   deps=[hm])
        hbs.append(hc)
        dep = dep + [hc]
    h_bcast_done = hbs[-1]

    h_st = []
    h_last = None
    for tt in range(4):
        t0 = tt * 512
        h_o, h_x6 = [], []
        for c in range(4):
            r0 = t0 + c * 128
            dd = [h_last] if h_last else []
            h_o.append(em.dma("sync", o_rows[c][:, :], o_d[r0:r0 + 128, :], deps=dd, chan=f"oo{c}"))
            h_x6.append(em.dma("sync", x7_sb[c][:, :], x6_d[r0:r0 + 128, :], deps=dd, chan=f"x6{c}"))
        h_ofm = [[None] * 4 for _ in range(8)]
        hp = h_last
        for c in range(4):
            for k in range(8):
                hm = em.op("tensor", lambda E, c=c, k=k: E.transpose(
                    pt[:, :], o_rows[c][:, k * 128:(k + 1) * 128], ident[:, :]),
                    deps=[h_o[c], h_id] + ([hp] if hp else []))
                hc = em.op("vector", lambda E, c=c, k=k: E.tensor_copy(
                    o_fm[k][:, c * 128:(c + 1) * 128], pt[:, :]), deps=[hm])
                h_ofm[k][c] = hc
                hp = hc
        for c in range(4):
            for fh in range(2):
                def ymm(E, c=c, fh=fh):
                    ins = None
                    for k in range(8):
                        ins = E.matmul(ps[c][:, :], o_fm[k][:, c * 128:(c + 1) * 128],
                                       po_sb[:, k * 1024 + fh * 512: k * 1024 + fh * 512 + 512],
                                       start=(k == 0), stop=(k == 7))
                    return ins
                dd = [h_ofm[k][c] for k in range(8)] + [h_po]
                if fh == 1:
                    dd.append(h_add)
                hmm = em.op("tensor", ymm, deps=dd)
                h_add = em.op("vector", lambda E, c=c, fh=fh: E.tensor_add(
                    x7_sb[c][:, fh * 512:(fh + 1) * 512],
                    x7_sb[c][:, fh * 512:(fh + 1) * 512], ps[c][:, :]),
                    deps=[hmm, h_x6[c]])
            cc = cols[:, c * 8:(c + 1) * 8]
            hst_ = emit_ln_stats(L, x7_sb[c][:, :], sq2[:, :], cc,
                                 deps=[h_add] + ([hfm_prev] if tt + c > 0 else []))
            hap = emit_ln_apply(L, x7_sb[c][:, :], lnb[:, :], cc,
                                deps=[hst_] + ([hfm_prev] if tt + c > 0 else []))
            ham = em.op("vector", lambda E: E.reduce_max(
                amx[:, 0:1], lnb[:, :], axis=mybir.AxisListType.X,
                apply_absolute_value=True), deps=[hap])
            ha2 = em.op("vector", lambda E: E.tensor_scalar(
                amx[:, 1:2], amx[:, 0:1], 1e-6, 1.0 / 127.0, ALU.max, ALU.mult), deps=[ham])
            ha3 = em.op("vector", lambda E: E.reciprocal(amx[:, 2:3], amx[:, 1:2]), deps=[ha2])
            hq1 = em.op("scalar", lambda E: E.activation(
                lnb[:, :], lnb[:, :], AF.Copy, scale=amx[:, 2:3]), deps=[ha3])
            hq2 = em.op("vector", lambda E: E.tensor_scalar(
                lnb[:, :], lnb[:, :], RND_C, RND_C, ALU.add, ALU.subtract), deps=[hq1])
            hq3 = em.op("vector", lambda E: E.tensor_scalar(
                xq_tm[:, :], lnb[:, :], 127.0, -128.0, ALU.min, ALU.max), deps=[hq2])
            hp2 = hq3
            for k in range(8):
                hm = em.op("tensor", lambda E, k=k: E.transpose(
                    pt[:, :], xq_tm[:, k * 128:(k + 1) * 128], ident[:, :]), deps=[hp2, h_id])
                hc = em.op("vector", lambda E, k=k: E.tensor_copy(
                    xq_fm[k][:, 0:128], pt[:, :]), deps=[hm])
                hp2 = hc
            hfm_prev = hp2
            for fh in range(2):
                def qmm(E, fh=fh):
                    ins = None
                    for k in range(8):
                        ins = E.matmul(ps[4][:, :], xq_fm[k][:, 0:128],
                                       wq_bf[:, k * 1024 + fh * 512: k * 1024 + fh * 512 + 512],
                                       start=(k == 0), stop=(k == 7))
                    return ins
                dd2 = [hp2, h_wqb] + ([h_o2] if tt + c + fh > 0 else [])
                hmm = em.op("tensor", qmm, deps=dd2)
                hdq = em.op("vector", lambda E, c=c, fh=fh: E.scalar_tensor_tensor(
                    out_sb[c][:, fh * 512:(fh + 1) * 512], ps[4][:, :], amx[:, 1:2],
                    wbc_sc[:, fh * 512:(fh + 1) * 512], ALU.mult, ALU.mult),
                    deps=[hmm, h_bcast_done])
                hb2 = em.op("vector", lambda E, c=c, fh=fh: E.tensor_add(
                    out_sb[c][:, fh * 512:(fh + 1) * 512],
                    out_sb[c][:, fh * 512:(fh + 1) * 512],
                    wbc_bi[:, fh * 512:(fh + 1) * 512]), deps=[hdq])
                h_o2 = em.op("vector", lambda E, c=c, fh=fh: E.tensor_add(
                    out_sb[c][:, fh * 512:(fh + 1) * 512],
                    out_sb[c][:, fh * 512:(fh + 1) * 512],
                    x7_sb[c][:, fh * 512:(fh + 1) * 512]), deps=[hb2, h_add])
            h_st.append(em.dma("sync", x8_d[t0 + c * 128: t0 + (c + 1) * 128, :],
                               out_sb[c][:, :], deps=[h_o2], chan="st"))
        h_last = h_st[-1]
        for e in ENGS:
            em.barrier(e, h_st[-4:])
    for e in ENGS:
        em.barrier(e, h_st)
    em.emit()
    return nc



def build_l3a():
    nc = bass.Bass("TRN2", target_bir_lowering=False)
    S = 4096
    x_d = nc.dram_tensor("xb", [S, 1024], F32, kind="ExternalInput").ap()
    pq_d = nc.dram_tensor("pq4", [4, 1024, 128], F32, kind="ExternalInput").ap()
    pk_d = nc.dram_tensor("pk", [1024, 128], F32, kind="ExternalInput").ap()
    pv_d = nc.dram_tensor("pv", [1024, 128], F32, kind="ExternalInput").ap()
    o_d = nc.dram_tensor("ofm", [512, S], BF16, kind="ExternalOutput").ap()

    em = Em(nc)
    L = Lib(em)
    ident, h_id = L.ident128()
    ones_col, h_oc = L.ones_col()
    ones_row, h_or = L.ones_rowf()
    sb = L.sb

    psS = [nc.alloc_psum_tensor(f"psS{i}", [128, 1024], F32) for i in range(2)]
    psO = nc.alloc_psum_tensor("psO", [128, 512], F32)
    psD = nc.alloc_psum_tensor("psD", [1, 512], F32)
    psB = nc.alloc_psum_tensor("psB", [128, 512], F32)
    pt = nc.alloc_psum_tensor("pt3", [128, 128], BF16)

    pq_sb = sb([128, 4 * 8 * 128], BF16, "pq_sb")
    pkv_sb = sb([128, 8 * 256], BF16, "pkv_sb")
    xn_fm = [sb([128, S], BF16, f"x3f{k}") for k in range(8)]
    q_sb = [sb([128, S], BF16, f"q{h}") for h in range(4)]
    k_sb = sb([128, S], BF16, "k_sb")
    v_tm = sb([128, 32 * 128], BF16, "v_tm")
    x_sb = sb([128, 1024], F32, "x_sb3")
    xn_tm = sb([128, 1024], BF16, "xn3")
    pexp = sb([128, 1024], BF16, "pexp")
    den = sb([1, 512], F32, "den")
    dbc = sb([128, 512], F32, "dbc")
    ofin = sb([128, 512], BF16, "ofin")
    sq2 = sb([128, 1024], BF16, "sq24")
    cols = sb([128, 8], F32, "cols4")

    shiftc, h_sh = L.const(np.full((128, 1), -20.0, np.float32), "shiftc")
    h_pq = em.dma("gpsimd", pq_sb.ap().rearrange("p (h k m) -> p h k m", h=4, k=8),
                  pq_d.rearrange("h (k p) m -> p h k m", p=128), chan="pq")
    h_pk = em.dma("gpsimd", pkv_sb.ap().rearrange("p (k m) -> p k m", k=8)[:, :, 0:128],
                  pk_d.rearrange("(k p) m -> p k m", p=128), chan="pk")
    h_pv = em.dma("gpsimd", pkv_sb.ap().rearrange("p (k m) -> p k m", k=8)[:, :, 128:256],
                  pv_d.rearrange("(k p) m -> p k m", p=128), chan="pk", deps=[h_pk])

    h_q = [[None] * 8 for _ in range(4)]
    h_k = [None] * 8
    h_v = [None] * 32
    hfm_c = [None] * 32
    hprev = None
    for c in range(32):
        h_x = em.dma("sync", x_sb[:, :], x_d[c * 128:(c + 1) * 128, :],
                     deps=[hprev] if hprev else [], chan="x")
        hst_ = emit_ln_stats(L, x_sb[:, :], sq2[:, :], cols[:, 0:4],
                             deps=[h_x] + ([hprev] if hprev else []))
        hap = emit_ln_apply(L, x_sb[:, :], xn_tm[:, :], cols[:, 0:4],
                            deps=[hst_] + ([hprev] if hprev else []))
        hp = hap
        for k in range(8):
            hm = em.op("tensor", lambda E, k=k: E.transpose(
                pt[:, :], xn_tm[:, k * 128:(k + 1) * 128], ident[:, :]), deps=[hp, h_id])
            hc = em.op("vector", lambda E, k=k, c=c: E.tensor_copy(
                xn_fm[k][:, c * 128:(c + 1) * 128], pt[:, :]), deps=[hm])
            hp = hc
        hprev = hp
        hfm_c[c] = hp
        def vmm(E, c=c):
            ins = None
            for k in range(8):
                ins = E.matmul(psB[:, 0:128], xn_fm[k][:, c * 128:(c + 1) * 128],
                               pkv_sb[:, k * 256 + 128: k * 256 + 256],
                               start=(k == 0), stop=(k == 7))
            return ins
        hvm = em.op("tensor", vmm, deps=[hp, h_pv] + ([h_v[c - 1]] if c else []))
        h_v[c] = em.op("vector", lambda E, c=c: E.tensor_copy(
            v_tm[:, c * 128:(c + 1) * 128], psB[:, 0:128]), deps=[hvm])
    hkq_prev = None
    for t5 in range(8):
        def kmm(E, t5=t5):
            ins = None
            for k in range(8):
                ins = E.matmul(psO[:, :], pkv_sb[:, k * 256: k * 256 + 128],
                               xn_fm[k][:, t5 * 512:(t5 + 1) * 512],
                               start=(k == 0), stop=(k == 7))
            return ins
        hkm = em.op("tensor", kmm, deps=[hfm_c[t5 * 4 + 3], h_pk] +
                    ([hkq_prev] if hkq_prev else []))
        h_k[t5] = em.op("vector", lambda E, t5=t5: E.tensor_copy(
            k_sb[:, t5 * 512:(t5 + 1) * 512], psO[:, :]), deps=[hkm])
        hkq_prev = h_k[t5]
        for h in range(4):
            def qmm(E, t5=t5, h=h):
                ins = None
                for k in range(8):
                    ins = E.matmul(psO[:, :], pq_sb[:, (h * 8 + k) * 128:(h * 8 + k) * 128 + 128],
                                   xn_fm[k][:, t5 * 512:(t5 + 1) * 512],
                                   start=(k == 0), stop=(k == 7))
                return ins
            hqm = em.op("tensor", qmm, deps=[hfm_c[t5 * 4 + 3], h_pq, hkq_prev])
            h_q[h][t5] = em.op("vector", lambda E, t5=t5, h=h: E.tensor_copy(
                q_sb[h][:, t5 * 512:(t5 + 1) * 512], psO[:, :]), deps=[hqm])
            hkq_prev = h_q[h][t5]

    SHIFT = 20.0
    h_store = []
    h_oev_prev = hkq_prev
    h_exp = [None, None]
    for h in range(4):
        for qt in range(8):
            hden0 = em.op("vector", lambda E: E.memset(den[:1, :], 0.0),
                          deps=[h_oev_prev])
            h_omm = None
            h_dmm = None
            for mg in range(16):
                pb = psS[mg % 2]
                def smm(E, h=h, qt=qt, mg=mg, pb=pb):
                    ins = []
                    for half in range(2):
                        m0 = mg * 256 + half * 128
                        ins.append(E.matmul(pb[:, half * 512:(half + 1) * 512],
                                            k_sb[:, m0:m0 + 128],
                                            q_sb[h][:, qt * 512:(qt + 1) * 512],
                                            start=True, stop=True))
                    return ins
                dd = [h_q[h][qt], h_k[7]] + ([h_exp[mg % 2]] if h_exp[mg % 2] else [])
                hsm = em.op("tensor", smm, deps=dd)
                hex_ = em.op("scalar", lambda E, pb=pb: E.activation(
                    pexp[:, :], pb[:, :], AF.Exp, bias=shiftc[:, 0:1]), deps=[hsm, h_sh] +
                    ([h_omm] if h_omm else []) + ([h_dmm] if h_dmm else []))
                h_exp[mg % 2] = hex_
                def omm(E, mg=mg):
                    ins = []
                    for half in range(2):
                        m0 = mg * 2 + half
                        ins.append(E.matmul(psO[:, :], v_tm[:, m0 * 128:(m0 + 1) * 128],
                                            pexp[:, half * 512:(half + 1) * 512].rearrange(
                                                "p (a b) -> p a b", a=1) if False else
                                            pexp[:, half * 512:(half + 1) * 512],
                                            start=(mg == 0 and half == 0),
                                            stop=(mg == 15 and half == 1)))
                    return ins
                h_omm = em.op("tensor", omm, deps=[hex_, h_v[31]] +
                              ([h_oev_prev] if mg == 0 else []))
                def dmm(E, mg=mg):
                    ins = []
                    for half in range(2):
                        ins.append(E.matmul(psD[:1, :], ones_col[:, 0:1],
                                            pexp[:, half * 512:(half + 1) * 512],
                                            start=(mg == 0 and half == 0),
                                            stop=(mg == 15 and half == 1)))
                    return ins
                h_dmm = em.op("tensor", dmm, deps=[hex_, h_oc])
            hcd = em.op("vector", lambda E: E.tensor_copy(den[:1, :], psD[:1, :]),
                        deps=[h_dmm, hden0])
            hrd = em.op("vector", lambda E: E.reciprocal(den[:1, :], den[:1, :]), deps=[hcd])
            hbb = em.op("tensor", lambda E: E.matmul(
                psB[:, :], ones_row[:1, :], den[:1, :], start=True, stop=True),
                deps=[hrd, h_or])
            hbc = em.op("vector", lambda E: E.tensor_copy(dbc[:, :], psB[:, :]), deps=[hbb])
            hoe = em.op("vector", lambda E: E.tensor_mul(ofin[:, :], psO[:, :], dbc[:, :]),
                        deps=[hbc, h_omm])
            h_store.append(em.dma("sync", o_d[h * 128:(h + 1) * 128, qt * 512:(qt + 1) * 512],
                                  ofin[:, :], deps=[hoe], chan="st"))
            h_oev_prev = hoe
            em.barrier("scalar", [hoe])
            em.barrier("tensor", [h_store[-1]])
    for e in ENGS:
        em.barrier(e, h_store)
    em.emit()
    return nc



def build_l2a1(T=2048):
    nc = bass.Bass("TRN2", target_bir_lowering=False)
    x4_d = nc.dram_tensor("x4", [T, 1024], F32, kind="ExternalInput").ap()
    win_d = nc.dram_tensor("win", [1024, 4096], F32, kind="ExternalInput").ap()
    wabc_d = nc.dram_tensor("wabc", [2048, 48], F32, kind="ExternalInput").ap()
    xin_d = nc.dram_tensor("xin", [T, 2048], BF16, kind="ExternalOutput").ap()
    abc_d = nc.dram_tensor("abc", [T, 48], F32, kind="ExternalOutput").ap()

    em = Em(nc)
    L = Lib(em)
    ident, h_id = L.ident128()
    ps = [nc.alloc_psum_tensor(f"ps{i}", [128, 512], F32) for i in range(6)]
    pt = nc.alloc_psum_tensor("pt", [128, 128], BF16)
    sb = L.sb

    win_sb = sb([128, 8 * 4096], BF16, "win_sb")
    wabc_sb = sb([128, 16 * 48], BF16, "wabc_sb")
    x4_sb = [sb([128, 1024], F32, f"x4a{i}") for i in range(4)]
    xn_tm = sb([128, 1024], BF16, "xna")
    xn_fm = [sb([128, 512], BF16, f"xaf{k}") for k in range(8)]
    xin_sb = sb([128, 2048], BF16, "xin_sb")
    xin_fm = [sb([128, 128], BF16, f"xif{k}") for k in range(16)]
    sgg = sb([128, 512], BF16, "sgg")
    sgb = sb([128, 512], BF16, "sgb")
    abc_sb = sb([128, 48], F32, "abc_sb")
    sq2 = sb([128, 1024], BF16, "sq2a")
    cols = sb([128, 32], F32, "colsa")

    h_w = em.dma("gpsimd", win_sb.ap().rearrange("p (k m) -> p k m", k=8),
                 win_d.rearrange("(k p) m -> p k m", p=128), chan="win")
    h_wa = em.dma("gpsimd", wabc_sb.ap().rearrange("p (k m) -> p k m", k=16),
                  wabc_d.rearrange("(k p) m -> p k m", p=128), chan="wabc")

    h_st = []
    h_last = None
    h_gate_prev = [None, None]
    h_abc_prev = None
    h_xin_prev = None
    h_xin_store = None
    for tt in range(4):
        t0 = tt * 512
        h_x4 = []
        for c in range(4):
            dd = [h_last] if h_last else []
            h_x4.append(em.dma("sync", x4_sb[c][:, :], x4_d[t0 + c * 128: t0 + c * 128 + 128, :],
                               deps=dd, chan=f"x4{c}"))
        hfm = None
        for c in range(4):
            cc = cols[:, c * 8:(c + 1) * 8]
            hst_ = emit_ln_stats(L, x4_sb[c][:, :], sq2[:, :], cc, deps=[h_x4[c], hfm])
            hap = emit_ln_apply(L, x4_sb[c][:, :], xn_tm[:, :], cc, deps=[hst_, hfm])
            hp = hap
            for k in range(8):
                hm = em.op("tensor", lambda E, k=k: E.transpose(
                    pt[:, :], xn_tm[:, k * 128:(k + 1) * 128], ident[:, :]), deps=[hp, h_id])
                hc = em.op("vector", lambda E, k=k, c=c: E.tensor_copy(
                    xn_fm[k][:, c * 128:(c + 1) * 128], pt[:, :]), deps=[hm])
                hp = hc
            hfm = hp
        for c in range(4):
            h_xin_parts = []
            for half in range(4):
                def amm(E, c=c, half=half):
                    ins = None
                    for k in range(8):
                        ins = E.matmul(ps[0][:, :], xn_fm[k][:, c * 128:(c + 1) * 128],
                                       win_sb[:, k * 4096 + half * 512: k * 4096 + half * 512 + 512],
                                       start=(k == 0), stop=(k == 7))
                    return ins
                def bmm(E, c=c, half=half):
                    ins = None
                    for k in range(8):
                        ins = E.matmul(ps[1][:, :], xn_fm[k][:, c * 128:(c + 1) * 128],
                                       win_sb[:, k * 4096 + 2048 + half * 512: k * 4096 + 2048 + half * 512 + 512],
                                       start=(k == 0), stop=(k == 7))
                    return ins
                ha = em.op("tensor", amm, deps=[hfm, h_w] +
                           ([h_gate_prev[0]] if h_gate_prev[0] else []))
                hb = em.op("tensor", bmm, deps=[hfm, h_w] +
                           ([h_gate_prev[1]] if h_gate_prev[1] else []))
                hs1 = em.op("scalar", lambda E: E.activation(sgg[:, :], ps[0][:, :], AF.Silu),
                            deps=[ha] + ([h_gate_prev[1]] if h_gate_prev[1] else []) +
                            ([h_xin_prev] if 'yes' == 'yes' and h_xin_prev else []))
                hs2 = em.op("scalar", lambda E: E.activation(sgb[:, :], ps[1][:, :], AF.Sigmoid),
                            deps=[hb, hs1])
                h_gate_prev = [hs1, hs2]
                hx = em.op("vector", lambda E, half=half: E.tensor_mul(
                    xin_sb[:, half * 512:(half + 1) * 512], sgg[:, :], sgb[:, :]),
                    deps=[hs2] + ([h_abc_prev] if h_abc_prev else []) +
                    ([h_xin_store] if h_xin_store else []))
                h_xin_parts.append(hx)
                h_xin_prev = hx
            h_st.append(em.dma("sync", xin_d[t0 + c * 128: t0 + c * 128 + 128, :],
                               xin_sb[:, :], deps=h_xin_parts, chan="st"))
            h_xin_store = h_st[-1]
            hp2 = h_xin_parts[-1]
            for k in range(16):
                hm = em.op("tensor", lambda E, k=k: E.transpose(
                    pt[:, :], xin_sb[:, k * 128:(k + 1) * 128], ident[:, :]),
                    deps=[hp2, h_id] + h_xin_parts)
                hc = em.op("vector", lambda E, k=k: E.tensor_copy(
                    xin_fm[k][:, :], pt[:, :]), deps=[hm])
                hp2 = hc
            def cmm(E):
                ins = None
                for k in range(16):
                    ins = E.matmul(ps[2][:, 0:48], xin_fm[k][:, :],
                                   wabc_sb[:, k * 48:(k + 1) * 48],
                                   start=(k == 0), stop=(k == 15))
                return ins
            hcm = em.op("tensor", cmm, deps=[hp2, h_wa] +
                        ([h_abc_prev] if h_abc_prev else []))
            h_abc_prev = em.op("vector", lambda E: E.tensor_copy(
                abc_sb[:, :], ps[2][:, 0:48]),
                deps=[hcm] + ([h_abc_st] if tt + c > 0 else []))
            h_abc_st = em.dma("sync", abc_d[t0 + c * 128: t0 + c * 128 + 128, :],
                              abc_sb[:, :], deps=[h_abc_prev], chan="st2")
            h_st.append(h_abc_st)
        h_last = h_st[-1]
        for e in ENGS:
            em.barrier(e, h_st[-8:])
    for e in ENGS:
        em.barrier(e, h_st)
    em.emit()
    return nc



def build_l2a2():
    nc = bass.Bass("TRN2", target_bir_lowering=False)
    S, DH, N, C = 4096, 1024, 16, 64
    NCH = S // C
    xin_d = nc.dram_tensor("xinc", [S, DH], BF16, kind="ExternalInput").ap()
    abc_d = nc.dram_tensor("abc", [S, 48], F32, kind="ExternalInput").ap()
    osc_d = nc.dram_tensor("osc", [S, DH], F32, kind="ExternalOutput").ap()

    em = Em(nc)
    L = Lib(em)
    identf, h_idf = L.ident128f()
    ident, h_id = L.ident128()
    triu64, h_tu = L.const(np.triu(np.ones((64, 64), np.float32)).astype(BF), "triu64")
    sb = L.sb

    psP = nc.alloc_psum_tensor("psP", [64, 64], F32)
    psY = [nc.alloc_psum_tensor(f"psY{i}", [64, 512], F32) for i in range(2)]
    psSc = nc.alloc_psum_tensor("psSc", [16, 1024], F32)
    ptB = nc.alloc_psum_tensor("ptB", [64, 16], BF16)
    ptf = nc.alloc_psum_tensor("ptf2", [128, 128], F32)

    abc_tm = sb([128, 48], F32, "abc_tm")
    a_fm = sb([16, S], F32, "a_fm")
    b_fm = sb([16, S], F32, "b_fm")
    c_fm = sb([16, S], F32, "c_fm")
    cum = sb([16, S], F32, "cum_fm")
    fplus = sb([16, S], F32, "fplus")
    bpp = sb([16, S], BF16, "bpp")
    bps = sb([16, S], BF16, "bps")
    cpp = sb([16, S], BF16, "cpp")
    gam = sb([16, NCH], F32, "gam")
    btm = sb([64, 16], BF16, "btm")
    p_sb = sb([64, 64], BF16, "p_sb")
    s_run = sb([16, DH], F32, "s_run")
    s_bf = sb([16, DH], BF16, "s_bf")
    xch = [sb([64, DH], BF16, f"xch{i}") for i in range(2)]
    ybuf = [sb([64, DH], F32, f"yb{i}") for i in range(2)]

    h_abc = []
    hprev = None
    for i in range(32):
        hld = em.dma("sync", abc_tm[:, :], abc_d[i * 128:(i + 1) * 128, :],
                     deps=[hprev] if hprev else [], chan="abc")
        hq = hld
        for part, dst in ((0, a_fm), (16, b_fm), (32, c_fm)):
            hm = em.op("tensor", lambda E, part=part: E.transpose(
                ptf[0:16, :], abc_tm[:, part:part + 16], identf[:, :]),
                deps=[hld, h_idf, hq])
            hq = em.op("vector", lambda E, i=i, dst=dst: E.tensor_copy(
                dst[:, i * 128:(i + 1) * 128], ptf[0:16, 0:128]), deps=[hm])
        hprev = hq
        h_abc.append(hq)
    h_abc_done = hprev

    h1 = em.op("vector", lambda E: E.tensor_scalar(
        a_fm[:, :], a_fm[:, :], -5.0, 0.0, ALU.max, ALU.min), deps=[h_abc_done])
    h2 = em.op("scalar", lambda E: E.activation(a_fm[:, :], a_fm[:, :], AF.Exp), deps=[h1])
    h3 = em.op("vector", lambda E: E.tensor_scalar(
        a_fm[:, :], a_fm[:, :], -1.0, None, ALU.mult), deps=[h2])
    h_cum = []
    hc = h3
    for ch in range(NCH):
        hc = em.op("vector", lambda E, ch=ch: E.tensor_tensor_scan(
            cum[:, ch * C:(ch + 1) * C], a_fm[:, ch * C:(ch + 1) * C],
            a_fm[:, ch * C:(ch + 1) * C], 0.0, ALU.add, ALU.bypass), deps=[hc])
        h_cum.append(hc)
    hfp = em.op("scalar", lambda E: E.activation(fplus[:, :], cum[:, :], AF.Exp), deps=[hc])
    hcpp = em.op("vector", lambda E: E.tensor_mul(cpp[:, :], c_fm[:, :], fplus[:, :]),
                 deps=[hfp])
    hgam = em.op("vector", lambda E: E.tensor_copy(
        gam[:, :], fplus.ap().rearrange("p (c t) -> p c t", t=C)[:, :, C - 1:C]),
        deps=[hfp])
    hfm_ = em.op("scalar", lambda E: E.activation(fplus[:, :], cum[:, :], AF.Exp,
                                                  scale=-1.0), deps=[hcpp, hgam])
    hbpp = em.op("vector", lambda E: E.tensor_mul(bpp[:, :], b_fm[:, :], fplus[:, :]),
                 deps=[hfm_])
    hee = em.op("vector", lambda E: E.tensor_copy(
        fplus.ap().rearrange("p (c t) -> p c t", t=C),
        cum.ap().rearrange("p (c t) -> p c t", t=C)[:, :, C - 1:C]
           .broadcast_to((16, NCH, C))), deps=[hbpp])
    hde = em.op("vector", lambda E: E.tensor_sub(fplus[:, :], fplus[:, :], cum[:, :]),
                deps=[hee])
    hdx = em.op("scalar", lambda E: E.activation(fplus[:, :], fplus[:, :], AF.Exp),
                deps=[hde])
    hbps = em.op("vector", lambda E: E.tensor_mul(bps[:, :], b_fm[:, :], fplus[:, :]),
                 deps=[hdx])
    hs0 = em.op("vector", lambda E: E.memset(s_run[:, :], 0.0))
    hs0b = em.op("vector", lambda E: E.memset(s_bf[:, :], 0.0), deps=[hs0])
    hs0 = hs0b

    h_store = []
    h_px = [None, None]
    h_supd = hs0
    h_yev = [None, None]
    h_btm_prev = None
    h_mask_prev = None
    for ch in range(NCH):
        bb = ch % 2
        hx = em.dma("sync", xch[bb][:, :], xin_d[ch * C:(ch + 1) * C, :],
                    deps=([h_px[bb]] if h_px[bb] else []) + [h_supd], chan=f"x{bb}")
        hbt = em.op("tensor", lambda E, ch=ch: E.transpose(
            ptB[:, :], bps[:, ch * C:(ch + 1) * C], ident[0:16, 0:16]),
            deps=[hbps, h_id] + ([h_btm_prev] if h_btm_prev else []))
        hbtc = em.op("vector", lambda E: E.tensor_copy(btm[:, :], ptB[:, :]), deps=[hbt])
        h_btm_prev = hbtc
        hpm = em.op("tensor", lambda E, ch=ch: E.matmul(
            psP[:, :], bpp[:, ch * C:(ch + 1) * C], cpp[:, ch * C:(ch + 1) * C],
            start=True, stop=True), deps=[hbpp, hcpp] +
            ([h_mask_prev] if h_mask_prev else []))
        h_mask_prev = em.op("vector", lambda E: E.tensor_mul(
            p_sb[:, :], psP[:, :], triu64[0:64, 0:64]), deps=[hpm, h_tu])
        h_ymm = []
        for fh in range(2):
            def ymm(E, ch=ch, fh=fh, bb=bb):
                i1 = E.matmul(psY[fh][:, :], p_sb[:, :],
                              xch[bb][:, fh * 512:(fh + 1) * 512], start=True, stop=False)
                i2 = E.matmul(psY[fh][:, :], cpp[:, ch * C:(ch + 1) * C],
                              s_bf[0:16, fh * 512:(fh + 1) * 512], start=False, stop=True)
                return [i1, i2]
            h_ymm.append(em.op("tensor", ymm, deps=[h_mask_prev, hx, h_supd] +
                               ([h_yev[fh]] if h_yev[fh] else [])))
        def scmm(E, ch=ch, bb=bb):
            i1 = E.matmul(psSc[:, 0:512], btm[:, :], xch[bb][:, 0:512],
                          start=True, stop=True)
            i2 = E.matmul(psSc[:, 512:1024], btm[:, :], xch[bb][:, 512:1024],
                          start=True, stop=True)
            return [i1, i2]
        hscm = em.op("tensor", scmm, deps=[hbtc, hx, h_supd])
        h_supd = em.op("vector", lambda E, ch=ch: E.scalar_tensor_tensor(
            s_run[:, :], s_run[:, :], gam[:, ch:ch + 1], psSc[:, :],
            ALU.mult, ALU.add), deps=[hscm, hgam] + h_ymm)
        h_supd = em.op("vector", lambda E: E.tensor_copy(s_bf[:, :], s_run[:, :]),
                       deps=[h_supd])
        for fh in range(2):
            h_yev[fh] = em.op("vector", lambda E, fh=fh, bb=bb: E.tensor_copy(
                ybuf[bb][:, fh * 512:(fh + 1) * 512], psY[fh][:, :]), deps=[h_ymm[fh]])
        h_store.append(em.dma("sync", osc_d[ch * C:(ch + 1) * C, :], ybuf[bb][:, :],
                              deps=[h_yev[0], h_yev[1]], chan=f"st{bb}"))
        h_px[bb] = h_store[-1]
    for e in ENGS:
        em.barrier(e, h_store[-2:])
    em.emit()
    return nc



def build_l0a():
    nc = bass.Bass("TRN2", target_bir_lowering=False)
    S, C, NCH, NH = 4096, 128, 32, 16
    x_d = nc.dram_tensor("xb", [S, 1024], F32, kind="ExternalInput").ap()
    prew_d = nc.dram_tensor("prew", [1, 1024], F32, kind="ExternalInput").ap()
    winz_d = nc.dram_tensor("winz", [1024, 1024], F32, kind="ExternalInput").ap()
    winxbc_d = nc.dram_tensor("winxbc", [1024, 1280], F32, kind="ExternalInput").ap()
    windt_d = nc.dram_tensor("windt", [1024, 16], F32, kind="ExternalInput").ap()
    convw_d = nc.dram_tensor("convw", [128, 40], F32, kind="ExternalInput").ap()
    convb_d = nc.dram_tensor("convb", [128, 10], F32, kind="ExternalInput").ap()
    dtb_d = nc.dram_tensor("dtb", [16, 1], F32, kind="ExternalInput").ap()
    alog_d = nc.dram_tensor("alog", [16, 1], F32, kind="ExternalInput").ap()
    drep_d = nc.dram_tensor("drep", [1, 1024], F32, kind="ExternalInput").ap()
    yg_d = nc.dram_tensor("ygh", [S, 1024], BF16, kind="ExternalOutput").ap()
    xh_dr = nc.dram_tensor("xh_dr", [8, 128, S], BF16)
    sz_dr = nc.dram_tensor("sz_dr", [S, 1024], BF16)
    cum_dr = nc.dram_tensor("cum_dr", [16, S], F32)
    fp_dr = nc.dram_tensor("fp_dr", [16, S], BF16)
    gam_dr = nc.dram_tensor("gam_dr", [NCH, 16], F32)

    em = Em(nc)
    L = Lib(em)
    ident, h_id = L.ident128()
    identf, h_idf = L.ident128f()
    ones_row, h_or = L.ones_row()
    ones_rowf, h_orf = L.ones_rowf()
    trilI, h_tl = L.const(np.triu(np.ones((128, 128), np.float32)).astype(BF), "trilI")

    ps0 = nc.alloc_psum_tensor("ps0", [128, 512], F32)
    ps1 = nc.alloc_psum_tensor("ps1", [128, 512], F32)
    psY = nc.alloc_psum_tensor("psY", [128, 1024], F32)
    psZ = nc.alloc_psum_tensor("psZ", [128, 1024], F32)
    pt = nc.alloc_psum_tensor("pt", [128, 128], BF16)
    ptf = nc.alloc_psum_tensor("ptf", [128, 128], F32)

    sb = L.sb
    u_fm = [sb([128, 2048], BF16, f"ufm{k}") for k in range(8)]
    winz_sb = sb([128, 8 * 1024], BF16, "winz_sb")
    winx_sb = sb([128, 8 * 1280], BF16, "winx_sb")
    windt_sb = sb([128, 8 * 16], BF16, "windt_sb")
    convw_sb = sb([128, 40], F32, "convw_sb")
    convb_sb = sb([128, 10], F32, "convb_sb")
    dtb_sb = sb([16, 1], F32, "dtb_sb")
    alog_sb = sb([16, 4], F32, "alog_sb")
    wbc_pre = sb([128, 1024], BF16, "wbc_pre")
    wbc_D = sb([128, 1024], BF16, "wbc_D")
    rowb = sb([1, 1024], BF16, "rowb")
    x_sb = sb([128, 1024], F32, "x_sb")
    u_tm = sb([128, 1024], BF16, "u_tm")
    sq2 = sb([128, 1024], F32, "sq2")
    cols = sb([128, 8], F32, "cols")
    convt = sb([128, 2051], BF16, "convt")
    convacc = sb([128, 2048], F32, "convacc")
    convo = sb([128, 2048], BF16, "convo")
    tail = sb([128, 32], BF16, "tail")
    b_fm = sb([128, S], BF16, "b_fm")
    c_fm = sb([128, S], BF16, "c_fm")
    packf = sb([128, S], F32, "packf")
    dt_fm = packf.ap()[0:16, :]
    cum = packf.ap()[32:48, :]
    scr16 = packf.ap()[64:80, :]
    packb = sb([128, S], BF16, "packb")
    dt_fmb = packb.ap()[0:16, :]
    fp_fm = packb.ap()[32:48, :]
    dec_fm = packb.ap()[64:80, :]
    gcols = sb([16, NCH], F32, "gcols")
    gcT = sb([NCH, 16], F32, "gcT")
    xh_fm = [sb([128, 128], BF16, f"xhf{i}") for i in range(8)]
    xh_tm = sb([128, 1024], BF16, "xh_tm")
    xt_tm = sb([128, 1024], BF16, "xt_tm")
    xd_tm = sb([128, 1024], BF16, "xd_tm")
    cum_tm = sb([128, 16], F32, "cum_tm")
    dt_tm = sb([128, 16], BF16, "dt_tm")
    dec_tm = sb([128, 16], BF16, "dec_tm")
    b_tm = sb([128, 128], BF16, "b_tm")
    gmm_sb = sb([128, 128], BF16, "gmm_sb")
    crow = sb([1, 2048], F32, "crow_l0")
    frow = sb([1, 2048], BF16, "frow")
    grow = sb([1, 16], F32, "grow")
    dsl = sb([128, 512], F32, "dsl")
    pmat = sb([128, 2048], BF16, "pmat")
    cpp = sb([128, 2048], BF16, "cpp_l0")
    zrun = sb([128, 1024], F32, "zrun")
    zbf = sb([128, 1024], BF16, "zbf")
    gbc = sb([128, 16], F32, "gbc")
    szc = sb([128, 1024], BF16, "szc")
    ygo = sb([128, 1024], F32, "ygo")
    ygb = sb([128, 1024], BF16, "ygb")

    h_wz = em.dma("gpsimd", winz_sb.ap().rearrange("p (k m) -> p k m", k=8),
                  winz_d.rearrange("(k p) m -> p k m", p=128), chan="wz")
    h_wx = em.dma("gpsimd", winx_sb.ap().rearrange("p (k m) -> p k m", k=8),
                  winxbc_d.rearrange("(k p) m -> p k m", p=128), chan="wx")
    h_wd = em.dma("gpsimd", windt_sb.ap().rearrange("p (k m) -> p k m", k=8),
                  windt_d.rearrange("(k p) m -> p k m", p=128), chan="wdt")
    h_cw = em.dma("gpsimd", convw_sb[:, :], convw_d[:, :], chan="cw")
    h_cb = em.dma("gpsimd", convb_sb[:, :], convb_d[:, :], chan="cw", deps=[h_cw])
    h_db = em.dma("gpsimd", dtb_sb[:, :], dtb_d[:, :], chan="db")
    h_al = em.dma("gpsimd", alog_sb[:, 0:1], alog_d[:, :], chan="db", deps=[h_db])
    h_pw = em.dma("gpsimd", rowb[:1, :], prew_d[:, :], chan="rw")
    hb_pre = []
    dep = [h_pw, h_or]
    for j in range(0, 1024, 512):
        hm = em.op("tensor", lambda E, j=j: E.matmul(
            ps0[:, :], ones_row[:1, :], rowb[:1, j:j + 512], start=True, stop=True), deps=dep)
        hc = em.op("vector", lambda E, j=j: E.tensor_copy(wbc_pre[:, j:j + 512], ps0[:, :]),
                   deps=[hm])
        hb_pre.append(hc)
        dep = dep + [hc]
    h_dw = em.dma("gpsimd", rowb[:1, :], drep_d[:, :], deps=hb_pre, chan="rw")
    hb_D = []
    dep = [h_dw]
    for j in range(0, 1024, 512):
        hm = em.op("tensor", lambda E, j=j: E.matmul(
            ps0[:, :], ones_row[:1, :], rowb[:1, j:j + 512], start=True, stop=True), deps=dep)
        hc = em.op("vector", lambda E, j=j: E.tensor_copy(wbc_D[:, j:j + 512], ps0[:, :]),
                   deps=[hm])
        hb_D.append(hc)
        dep = dep + [hc]
    h_a1 = em.op("scalar", lambda E: E.activation(alog_sb[:, 1:2], alog_sb[:, 0:1], AF.Exp),
                 deps=[h_al])
    h_a2 = em.op("vector", lambda E: E.tensor_scalar(
        alog_sb[:, 1:2], alog_sb[:, 1:2], -1.0, None, ALU.mult), deps=[h_a1])
    h_tail0 = em.op("vector", lambda E: E.memset(tail[:, :], 0.0))

    h_zst = None
    h_dtp = None
    h_ufm_last = None
    h_xload = None
    h_xst = [None] * 10
    h_tailw = h_tail0
    h_convdone = None
    h_sl_prev = None
    h_convo_user = None
    for half in range(2):
        t0 = half * 2048
        for cc_ in range(16):
            tok = t0 + cc_ * 128
            h_x = em.dma("sync", x_sb[:, :], x_d[tok:tok + 128, :],
                         deps=[h_xload] if h_xload else [], chan="x")
            co = cols[:, 0:8]
            h1 = em.op("vector", lambda E: E.reduce_sum(co[:, 0:1], x_sb[:, :],
                                                        axis=mybir.AxisListType.X),
                       deps=[h_x] + ([h_ufm_last] if h_ufm_last else []))
            h2 = em.op("scalar", lambda E: E.activation(sq2[:, :], x_sb[:, :], AF.Square),
                       deps=[h_x] + ([h_ufm_last] if h_ufm_last else []))
            h3 = em.op("vector", lambda E: E.reduce_sum(co[:, 1:2], sq2[:, :],
                                                        axis=mybir.AxisListType.X), deps=[h2])
            h4 = em.op("vector", lambda E: E.tensor_scalar(
                co[:, 0:1], co[:, 0:1], 1.0 / 1024.0, None, ALU.mult), deps=[h1])
            h5 = em.op("scalar", lambda E: E.activation(co[:, 2:3], co[:, 0:1], AF.Square),
                       deps=[h4])
            h6 = em.op("vector", lambda E: E.tensor_scalar(
                co[:, 1:2], co[:, 1:2], 1.0 / 1024.0, None, ALU.mult), deps=[h3])
            h7 = em.op("vector", lambda E: E.tensor_sub(co[:, 1:2], co[:, 1:2], co[:, 2:3]),
                       deps=[h5, h6])
            h8 = em.op("vector", lambda E: E.tensor_scalar(
                co[:, 1:2], co[:, 1:2], 1.000001, 1e-11, ALU.mult, ALU.add), deps=[h7])
            h9 = em.op("scalar", lambda E: E.activation(co[:, 2:3], co[:, 1:2], AF.Sqrt),
                       deps=[h8])
            h10 = em.op("vector", lambda E: E.reciprocal(co[:, 3:4], co[:, 2:3]), deps=[h9])
            h11 = em.op("vector", lambda E: E.tensor_mul(co[:, 4:5], co[:, 0:1], co[:, 3:4]),
                        deps=[h10])
            h12 = em.op("vector", lambda E: E.tensor_scalar(
                sq2[:, :], x_sb[:, :], co[:, 3:4], co[:, 4:5], ALU.mult, ALU.subtract),
                deps=[h11])
            h13 = em.op("vector", lambda E: E.tensor_mul(u_tm[:, :], sq2[:, :], wbc_pre[:, :]),
                        deps=[h12] + hb_pre)
            h_xload = h13
            hp = h13
            for k in range(8):
                hm = em.op("tensor", lambda E, k=k: E.transpose(
                    pt[:, :], u_tm[:, k * 128:(k + 1) * 128], ident[:, :]), deps=[hp, h_id])
                hcp = em.op("vector", lambda E, k=k, cc_=cc_: E.tensor_copy(
                    u_fm[k][:, cc_ * 128:(cc_ + 1) * 128], pt[:, :]), deps=[hm])
                hp = hcp
            h_ufm_last = hp
        for cc_ in range(16):
            tok = t0 + cc_ * 128
            hszs = []
            for fh in range(2):
                pz = ps0 if fh == 0 else ps1
                def zmm(E, cc_=cc_, fh=fh, pz=pz):
                    ins = None
                    for k in range(8):
                        ins = E.matmul(pz[:, :], u_fm[k][:, cc_ * 128:(cc_ + 1) * 128],
                                       winz_sb[:, k * 1024 + fh * 512: k * 1024 + fh * 512 + 512],
                                       start=(k == 0), stop=(k == 7))
                    return ins
                hmz = em.op("tensor", zmm, deps=[h_ufm_last, h_wz] +
                            ([h_zst] if h_zst else []) + hb_D)
                hszs.append(em.op("scalar", lambda E, fh=fh, pz=pz: E.activation(
                    ygb[:, fh * 512:(fh + 1) * 512], pz[:, :], AF.Silu),
                    deps=[hmz] + ([h_zst] if h_zst else [])))
            h_zst = em.dma("sync", sz_dr[tok:tok + 128, :], ygb[:, :], deps=hszs, chan="sz")
        h_dts = []
        for t5 in range(4):
            def dtmm(E, t5=t5):
                ins = None
                for k in range(8):
                    ins = E.matmul(ps0[0:16, :], windt_sb[:, k * 16:(k + 1) * 16],
                                   u_fm[k][:, t5 * 512:(t5 + 1) * 512],
                                   start=(k == 0), stop=(k == 7))
                return ins
            hdt = em.op("tensor", dtmm, deps=[h_ufm_last, h_wd, h_zst] +
                        ([h_dts[-1]] if h_dts else []) + ([h_dtp] if h_dtp else []))
            h_dts.append(em.op("vector", lambda E, t0=t0, t5=t5: E.tensor_scalar(
                dt_fm[:, t0 + t5 * 512: t0 + (t5 + 1) * 512], ps0[0:16, :],
                dtb_sb[:, 0:1], None, ALU.add), deps=[hdt, h_db]))
        hd2 = em.op("scalar", lambda E, t0=t0: E.activation(
            scr16[:, t0:t0 + 2048], dt_fm[:, t0:t0 + 2048], AF.Exp), deps=h_dts)
        h_dtp = em.op("scalar", lambda E, t0=t0: E.activation(
            dt_fm[:, t0:t0 + 2048], scr16[:, t0:t0 + 2048], AF.Ln, bias=1.0), deps=[hd2])
        for fc in range(10):
            h_cps = []
            for t5 in range(4):
                pz = ps0 if t5 % 2 == 0 else ps1
                def xmm(E, fc=fc, t5=t5, pz=pz):
                    ins = None
                    for k in range(8):
                        ins = E.matmul(pz[:, :],
                                       winx_sb[:, k * 1280 + fc * 128: k * 1280 + fc * 128 + 128],
                                       u_fm[k][:, t5 * 512:(t5 + 1) * 512],
                                       start=(k == 0), stop=(k == 7))
                    return ins
                ddx = [h_ufm_last, h_wx, h_dtp]
                if h_cps:
                    ddx.append(h_cps[-1])
                if h_xst[fc]:
                    ddx.append(h_xst[fc])
                hmx = em.op("tensor", xmm, deps=ddx)
                h_cps.append(em.op("vector", lambda E, t5=t5, pz=pz: E.tensor_copy(
                    convt[:, 3 + t5 * 512: 3 + (t5 + 1) * 512], pz[:, :]),
                    deps=[hmx] + ([h_tailw] if t5 == 0 else []) +
                    ([h_convdone] if h_convdone else [])))
            hpre = em.op("vector", lambda E, fc=fc: E.tensor_copy(
                convt[:, 0:3], tail[:, fc * 3:fc * 3 + 3]), deps=h_cps + [h_tailw])
            h_tailw = em.op("vector", lambda E, fc=fc: E.tensor_copy(
                tail[:, fc * 3:fc * 3 + 3], convt[:, 2048:2051]), deps=[hpre])
            hc0 = em.op("vector", lambda E, fc=fc: E.tensor_scalar(
                convacc[:, :], convt[:, 0:2048], convw_sb[:, fc * 4:fc * 4 + 1],
                None, ALU.mult), deps=[hpre, h_cw, h_tailw] +
                ([h_sl_prev] if h_sl_prev else []))
            for j in range(1, 4):
                hc0 = em.op("vector", lambda E, fc=fc, j=j: E.scalar_tensor_tensor(
                    convacc[:, :], convt[:, j:j + 2048], convw_sb[:, fc * 4 + j:fc * 4 + j + 1],
                    convacc[:, :], ALU.mult, ALU.add), deps=[hc0])
            h_convdone = hc0
            hsl = em.op("scalar", lambda E, fc=fc: E.activation(
                convo[:, :], convacc[:, :], AF.Silu, bias=convb_sb[:, fc:fc + 1]),
                deps=[hc0, h_cb] + ([h_sl_prev] if h_sl_prev else []) +
                ([h_convo_user] if h_convo_user else []))
            h_sl_prev = hsl
            if fc < 8:
                h_xst[fc] = em.dma("sync", xh_dr.ap()[fc, :, t0:t0 + 2048], convo[:, :],
                                   deps=[hsl], chan=f"xst{fc}")
            else:
                dst = b_fm if fc == 8 else c_fm
                h_xst[fc] = em.op("vector", lambda E, dst=dst, t0=t0: E.tensor_copy(
                    dst[:, t0:t0 + 2048], convo[:, :]), deps=[hsl])
            h_convo_user = h_xst[fc]

    ha = em.op("vector", lambda E: E.tensor_scalar(
        scr16[:, :], dt_fm[:, :], alog_sb[:, 1:2], None, ALU.mult), deps=[h_dtp, h_a2])
    hcum = ha
    for ch in range(NCH):
        hcum = em.op("vector", lambda E, ch=ch: E.tensor_tensor_scan(
            cum[:, ch * C:(ch + 1) * C], scr16[:, ch * C:(ch + 1) * C],
            scr16[:, ch * C:(ch + 1) * C], 0.0, ALU.add, ALU.bypass), deps=[hcum])
    hfp = em.op("scalar", lambda E: E.activation(fp_fm[:, :], cum[:, :], AF.Exp), deps=[hcum])
    hgc0 = em.op("vector", lambda E: E.tensor_copy(
        gcols[:, :], cum.rearrange("p (c t) -> p c t", t=C)[:, :, C - 1:C]), deps=[hcum])
    hgc = em.op("scalar", lambda E: E.activation(gcols[:, :], gcols[:, :], AF.Exp), deps=[hgc0])
    hde = em.op("vector", lambda E: E.tensor_tensor(
        scr16.rearrange("p (c t) -> p c t", t=C),
        cum.rearrange("p (c t) -> p c t", t=C)[:, :, C - 1:C].broadcast_to((16, NCH, C)),
        cum.rearrange("p (c t) -> p c t", t=C), ALU.subtract),
        deps=[ha, hcum, hfp])
    hdxx = em.op("scalar", lambda E: E.activation(dec_fm[:, :], scr16[:, :], AF.Exp),
                 deps=[hde])
    hdtb = em.op("vector", lambda E: E.tensor_copy(dt_fmb[:, :], dt_fm[:, :]), deps=[h_dtp])
    hcd = em.dma("sync", cum_dr[:, :], cum[:, :], deps=[hcum, hgc0, hde], chan="cumd")
    hfd = em.dma("sync", fp_dr[:, :], fp_fm[:, :], deps=[hfp], chan="fpd")
    hT = em.op("tensor", lambda E: E.transpose(ptf[0:NCH, 0:16], gcols[:, :], identf[0:16, 0:16]),
               deps=[hgc, h_idf])
    hTc = em.op("vector", lambda E: E.tensor_copy(gcT[:, :], ptf[0:NCH, 0:16]), deps=[hT])
    hgd = em.dma("sync", gam_dr[:, :], gcT[:, :], deps=[hTc], chan="gamd")

    hz0 = em.op("vector", lambda E: E.memset(zrun[:, :], 0.0))
    hz0b = em.op("vector", lambda E: E.memset(zbf[:, :], 0.0), deps=[hz0])
    h_prev_yst = None
    h_crow = None
    h_frow = None
    h_grow = None
    h_xf = [None] * 8
    h_sz = None
    h_zupd = hz0b
    h_zbf = hz0b
    h_pm = None
    h_cpp = None
    h_ps1_last = None
    h_yev = [None, None, None]
    for ch in range(NCH):
        tok = ch * C
        for k in range(8):
            h_xf[k] = em.dma("sync", xh_fm[k][:, :], xh_dr.ap()[k, :, tok:tok + C],
                             deps=([h_xf[k]] if h_xf[k] else []) + [h_xst[k]],
                             chan=f"xf{k}")
        h_sz = em.dma("sync", szc[:, :], sz_dr[tok:tok + C, :],
                      deps=([h_sz] if h_sz else []) + [h_zst], chan="szl")
        h_crow = em.dma("sync", crow[:1, :].rearrange("p (h t) -> p h t", h=16),
                        cum_dr.ap()[:, tok:tok + C],
                        deps=([h_crow] if h_crow else []) + [hcd], chan="crowl")
        h_frow = em.dma("sync", frow[:1, :].rearrange("p (h t) -> p h t", h=16),
                        fp_dr.ap()[:, tok:tok + C],
                        deps=([h_frow] if h_frow else []) + [hfd], chan="frowl")
        h_grow = em.dma("sync", grow[:1, :], gam_dr[ch:ch + 1, :],
                        deps=([h_grow] if h_grow else []) + [hgd], chan="growl")
        hxt = h_prev_yst
        for k in range(8):
            hm = em.op("tensor", lambda E, k=k: E.transpose(
                pt[:, :], xh_fm[k][:, :], ident[:, :]),
                deps=[h_xf[k], h_id] + ([hxt] if hxt else []))
            hxt = em.op("vector", lambda E, k=k: E.tensor_copy(
                xh_tm[:, k * 128:(k + 1) * 128], pt[:, :]), deps=[hm])
        hm = em.op("tensor", lambda E, tok=tok: E.transpose(
            ptf[:, 0:16], cum[:, tok:tok + C], identf[32:48, 32:48]), deps=[hcum, h_idf, hxt])
        hct = em.op("vector", lambda E: E.tensor_copy(cum_tm[:, :], ptf[:, 0:16]), deps=[hm])
        hm = em.op("tensor", lambda E, tok=tok: E.transpose(
            pt[:, 0:16], dt_fmb[:, tok:tok + C], ident[0:16, 0:16]), deps=[hdtb, h_id, hct])
        hdtt = em.op("vector", lambda E: E.tensor_copy(dt_tm[:, :], pt[:, 0:16]), deps=[hm])
        hm = em.op("tensor", lambda E, tok=tok: E.transpose(
            pt[:, 0:16], dec_fm[:, tok:tok + C], ident[64:80, 64:80]), deps=[hdxx, h_id, hdtt])
        hdct = em.op("vector", lambda E: E.tensor_copy(dec_tm[:, :], pt[:, 0:16]), deps=[hm])
        hm = em.op("tensor", lambda E, tok=tok: E.transpose(
            pt[:, :], b_fm[:, tok:tok + C], ident[:, :]), deps=[h_xst[8], h_id, hdct])
        hbt = em.op("vector", lambda E: E.tensor_copy(b_tm[:, :], pt[:, :]), deps=[hm])
        hxt2 = em.op("gpsimd", lambda E: E.tensor_tensor(
            xt_tm.ap().rearrange("p (h q) -> p h q", h=16),
            xh_tm.ap().rearrange("p (h q) -> p h q", h=16),
            dt_tm.ap()[:, :, None].broadcast_to((128, 16, 64)),
            ALU.mult), deps=[hxt, hdtt, h_zupd])
        hxd = em.op("gpsimd", lambda E: E.tensor_tensor(
            xd_tm.ap().rearrange("p (h q) -> p h q", h=16),
            xt_tm.ap().rearrange("p (h q) -> p h q", h=16),
            dec_tm.ap()[:, :, None].broadcast_to((128, 16, 64)),
            ALU.mult), deps=[hxt2, hdct])
        hg1 = em.op("tensor", lambda E, tok=tok: E.matmul(
            ps0[:, 0:128], b_fm[:, tok:tok + C], c_fm[:, tok:tok + C],
            start=True, stop=True), deps=[h_xst[8], h_xst[9]] +
            ([h_ps1_last] if h_ps1_last else []))
        hg2 = em.op("vector", lambda E: E.tensor_mul(gmm_sb[:, :], ps0[:, 0:128],
                                                     trilI[:, :]), deps=[hg1, h_tl])
        hdm = []
        for j4 in range(4):
            hb1 = em.op("tensor", lambda E, j4=j4: E.matmul(
                ps1[:, :], ones_rowf[:1, :], crow[:1, j4 * 512:(j4 + 1) * 512],
                start=True, stop=True), deps=[h_crow, h_orf] +
                ([h_ps1_last] if h_ps1_last else []) + (hdm[-1:] if hdm else []))
            hs = em.op("vector", lambda E, j4=j4: E.tensor_tensor(
                dsl.ap().rearrange("p (h q) -> p h q", h=4),
                ps1.ap().rearrange("p (h q) -> p h q", h=4),
                cum_tm.ap()[:, j4 * 4:(j4 + 1) * 4][:, :, None]
                .broadcast_to((128, 4, 128)),
                ALU.subtract), deps=[hb1, hct] + ([h_pm] if h_pm and j4 == 0 else []) +
                (hdm[-1:] if hdm else []))
            hs2 = em.op("vector", lambda E: E.tensor_scalar(
                dsl[:, :], dsl[:, :], 0.0, None, ALU.min), deps=[hs])
            hdm.append(em.op("scalar", lambda E, j4=j4: E.activation(
                pmat[:, j4 * 512:(j4 + 1) * 512], dsl[:, :], AF.Exp),
                deps=[hs2] + ([h_pm] if h_pm and j4 == 0 else [])))
        h_pm = em.op("vector", lambda E: E.tensor_tensor(
            pmat.ap().rearrange("p (h q) -> p h q", h=16),
            pmat.ap().rearrange("p (h q) -> p h q", h=16),
            gmm_sb.ap()[:, None, :].broadcast_to((128, 16, 128)),
            ALU.mult), deps=hdm + [hg2])
        hcbs = []
        for j4 in range(4):
            hb1 = em.op("tensor", lambda E, j4=j4: E.matmul(
                ps1[:, :], ones_row[:1, :], frow[:1, j4 * 512:(j4 + 1) * 512],
                start=True, stop=True), deps=[h_frow, h_or, hdm[-1]] +
                ([h_cpp] if h_cpp and j4 == 0 else []) + (hcbs[-1:] if hcbs else []))
            hcbs.append(em.op("vector", lambda E, j4=j4, tok=tok: E.tensor_tensor(
                cpp.ap().rearrange("p (h q) -> p h q", h=16)[:, j4 * 4:(j4 + 1) * 4, :],
                ps1.ap().rearrange("p (h q) -> p h q", h=4),
                c_fm.ap()[:, tok:tok + C][:, None, :].broadcast_to((128, 4, 128)),
                ALU.mult), deps=[hb1, h_xst[9]] +
                ([h_cpp] if h_cpp and j4 == 0 else [])))
        h_cpp = hcbs[-1]
        hgb1 = em.op("tensor", lambda E: E.matmul(
            ps1[:, 0:16], ones_rowf[:1, :], grow[:1, :], start=True, stop=True),
            deps=[h_grow, h_orf] + hcbs)
        h_ps1_last = em.op("vector", lambda E: E.tensor_copy(gbc[:, :], ps1[:, 0:16]),
                           deps=[hgb1])
        hgbc = h_ps1_last
        def ymm(E):
            ins = []
            for h in range(16):
                ins.append(E.matmul(psY[:, h * 64:(h + 1) * 64],
                                    pmat[:, h * 128:(h + 1) * 128],
                                    xt_tm[:, h * 64:(h + 1) * 64],
                                    start=(h in (0, 8)), stop=False))
            for h in range(16):
                ins.append(E.matmul(psY[:, h * 64:(h + 1) * 64],
                                    cpp[:, h * 128:(h + 1) * 128],
                                    zbf[:, h * 64:(h + 1) * 64],
                                    start=False, stop=(h in (7, 15))))
            return ins
        hY = em.op("tensor", ymm, deps=[h_pm, h_cpp, hxt2, h_zbf] +
                   ([h_yev[0]] if h_yev[0] else []))
        def zmm2(E):
            ins = []
            for h in range(16):
                ins.append(E.matmul(psZ[:, h * 64:(h + 1) * 64], b_tm[:, :],
                                    xd_tm[:, h * 64:(h + 1) * 64],
                                    start=(h in (0, 8)), stop=(h in (7, 15))))
            return ins
        hZ = em.op("tensor", zmm2, deps=[hbt, hxd, h_zupd])
        hz1 = em.op("vector", lambda E: E.tensor_tensor(
            zrun.ap().rearrange("p (h q) -> p h q", h=16),
            zrun.ap().rearrange("p (h q) -> p h q", h=16),
            gbc.ap()[:, :, None].broadcast_to((128, 16, 64)),
            ALU.mult), deps=[hgbc, hY, h_zbf])
        h_zupd = em.op("vector", lambda E: E.tensor_add(zrun[:, :], zrun[:, :], psZ[:, :]),
                       deps=[hz1, hZ])
        h_zbf = em.op("vector", lambda E: E.tensor_copy(zbf[:, :], zrun[:, :]), deps=[h_zupd])
        he1 = em.op("gpsimd", lambda E: E.tensor_mul(ygo[:, :], xh_tm[:, :], wbc_D[:, :]),
                    deps=[hxt] + hb_D + ([h_yev[1]] if h_yev[1] else []))
        he2 = em.op("vector", lambda E: E.tensor_add(ygo[:, :], ygo[:, :], psY[:, :]),
                    deps=[he1, hY])
        he3 = em.op("vector", lambda E: E.tensor_mul(ygb[:, :], ygo[:, :], szc[:, :]),
                    deps=[he2, h_sz] + ([h_prev_yst] if h_prev_yst else []))
        h_yev = [he2, he1, he3]
        h_prev_yst = em.dma("sync", yg_d[tok:tok + C, :], ygb[:, :], deps=[he3], chan="ygst")
    for e in ENGS:
        em.barrier(e, [h_prev_yst])
    em.emit()
    return nc



def build_l1a():
    nc = bass.Bass("TRN2", target_bir_lowering=False)
    S, C, NCH = 4096, 128, 32
    x_d = nc.dram_tensor("xb", [S, 1024], F32, kind="ExternalInput").ap()
    wqk_d = nc.dram_tensor("wqk", [1024, 768], F32, kind="ExternalInput").ap()
    wvg_d = nc.dram_tensor("wvg", [1024, 1536], F32, kind="ExternalInput").ap()
    wab_d = nc.dram_tensor("wab", [1024, 6], F32, kind="ExternalInput").ap()
    convw_d = nc.dram_tensor("convw", [128, 48], F32, kind="ExternalInput").ap()
    dtb_d = nc.dram_tensor("dtb3", [3, 1], F32, kind="ExternalInput").ap()
    alog_d = nc.dram_tensor("alog3", [3, 1], F32, kind="ExternalInput").ap()
    nw_d = nc.dram_tensor("nwr", [1, 256], F32, kind="ExternalInput").ap()
    og_d = nc.dram_tensor("ogn", [S, 768], BF16, kind="ExternalOutput").ap()
    gate_dr = nc.dram_tensor("gate_dr1", [S, 768], BF16)
    qkv_dr = nc.dram_tensor("qkv_dr1", [1536, S], BF16)
    nsq_dr = nc.dram_tensor("nsq_dr1", [6, S], F32)
    cum_dr = nc.dram_tensor("cum_dr1", [3, S], F32)
    fp_dr = nc.dram_tensor("fp_dr1", [3, S], BF16)
    br_dr = nc.dram_tensor("br_dr1", [3, S], BF16)
    gam_dr = nc.dram_tensor("gam_dr1", [NCH, 3], F32)

    em = Em(nc)
    L = Lib(em)
    ident, h_id = L.ident128()
    identf, h_idf = L.ident128f()
    ones_row, h_or = L.ones_row()
    ones_rowf, h_orf = L.ones_rowf()
    ones_col, h_oc = L.ones_col()
    triuI, h_tui = L.const(np.triu(np.ones((128, 128), np.float32)).astype(BF), "triuI")
    triuSn, h_tsn = L.const((-np.triu(np.ones((128, 128), np.float32), 1)).astype(BF),
                            "triuSn")

    ps0 = nc.alloc_psum_tensor("ps0", [128, 512], F32)
    psG = nc.alloc_psum_tensor("psG", [128, 128], F32)
    psG2 = nc.alloc_psum_tensor("psG2", [128, 128], F32)
    pU = nc.alloc_psum_tensor("pU", [128, 256], F32)
    pO = nc.alloc_psum_tensor("pO", [128, 256], F32)
    pS = nc.alloc_psum_tensor("pS", [128, 256], F32)
    pt = nc.alloc_psum_tensor("pt", [128, 128], BF16)
    ptf = nc.alloc_psum_tensor("ptf", [128, 16], F32)

    sb = L.sb
    u_fm = [sb([128, 2048], BF16, f"ufm{k}") for k in range(8)]
    wqk_sb = sb([128, 8 * 768], BF16, "wqk_sb")
    wvg_sb = sb([128, 8 * 1536], BF16, "wvg_sb")
    wab_sb = sb([128, 8 * 6], BF16, "wab_sb")
    convw_sb = sb([128, 48], F32, "convw_sb")
    dtb_sb = sb([3, 1], F32, "dtb_sb")
    alog_sb = sb([3, 4], F32, "alog_sb")
    wbc_nw = sb([128, 256], F32, "wbc_nw")
    rowb = sb([1, 256], F32, "rowb1")
    x_sb = sb([128, 1024], F32, "x_sb1")
    u_tm = sb([128, 1024], BF16, "u_tm1")
    sq2 = sb([128, 1024], F32, "sq21")
    cols = sb([128, 8], F32, "cols1")
    convt = sb([128, 2051], BF16, "convt1")
    convacc = sb([128, 2048], F32, "convacc1")
    qkvh = sb([128, 2048], BF16, "qkvh")
    qsq = sb([128, 2048], BF16, "qsq")
    nrow = sb([1, 2048], F32, "nrow")
    tail1 = sb([128, 40], BF16, "tail1")
    og_sb = sb([128, 768], BF16, "og_sb1")
    packf1 = sb([128, S], F32, "packf1")
    a_fm = packf1.ap()[0:3, :]
    cum = packf1.ap()[32:35, :]
    packb1 = sb([128, S], BF16, "packb1")
    fp_fm = packb1.ap()[0:3, :]
    be_fm = packb1.ap()[32:35, :]
    gcols = sb([3, NCH], F32, "gcols1")
    gcT = sb([NCH, 3], F32, "gcT1")
    qfm_c = sb([128, 128], BF16, "qfm_c")
    kfm_c = sb([128, 128], BF16, "kfm_c")
    vfm_c = sb([128, 256], BF16, "vfm_c")
    v_tm = sb([128, 256], BF16, "v_tm1")
    b_sb = sb([128, 256], BF16, "b_sb1")
    u_sb = sb([128, 256], BF16, "u_sb1")
    dsl1 = sb([128, 128], F32, "dsl1")
    eF = sb([128, 128], F32, "eF")
    eT = sb([128, 128], BF16, "eT")
    gm1 = sb([128, 128], BF16, "gm1")
    gm2 = sb([128, 128], BF16, "gm2")
    gmq = sb([128, 128], BF16, "gmq")
    amat = sb([128, 128], BF16, "amat")
    bnx = sb([128, 128], BF16, "bnx")
    bnxT = sb([128, 128], BF16, "bnxT")
    mmt = sb([128, 128], BF16, "mmt")
    kgb = sb([128, 128], BF16, "kgb")
    qg = sb([128, 128], BF16, "qg")
    svs = [sb([128, 256], F32, f"sv{h}") for h in range(3)]
    svb = [sb([128, 256], BF16, f"svb{h}") for h in range(3)]
    oc = sb([128, 256], F32, "oc1")
    oc2 = sb([128, 256], F32, "oc21")
    gate_sb = sb([128, 768], BF16, "gate_sb")
    crow = sb([1, 384], F32, "crow1")
    frow = sb([1, 384], BF16, "frow1")
    brw = sb([1, 384], BF16, "brw1")
    nqk = sb([1, 768], F32, "nqk")
    nrst = sb([1, 768], F32, "nrst")
    grow = sb([1, 3], F32, "grow1")
    cum_tm = sb([128, 4], F32, "cum_tm1")
    b_tmc = sb([128, 4], F32, "b_tmc")
    gbc = sb([128, 4], F32, "gbc1")
    gcolx = sb([128, 8], F32, "gcolx")

    h_wqk = em.dma("gpsimd", wqk_sb.ap().rearrange("p (k m) -> p k m", k=8),
                   wqk_d.rearrange("(k p) m -> p k m", p=128), chan="wqk")
    h_wvg = em.dma("gpsimd", wvg_sb.ap().rearrange("p (k m) -> p k m", k=8),
                   wvg_d.rearrange("(k p) m -> p k m", p=128), chan="wvg")
    h_wab = em.dma("gpsimd", wab_sb.ap().rearrange("p (k m) -> p k m", k=8),
                   wab_d.rearrange("(k p) m -> p k m", p=128), chan="wab")
    h_cw = em.dma("gpsimd", convw_sb[:, :], convw_d[:, :], chan="cw1")
    h_db = em.dma("gpsimd", dtb_sb[:, :], dtb_d[:, :], chan="db1")
    h_al = em.dma("gpsimd", alog_sb[:, 0:1], alog_d[:, :], chan="db1", deps=[h_db])
    h_nw = em.dma("gpsimd", rowb[:1, :], nw_d[:, :], chan="nw1")
    hm0 = em.op("tensor", lambda E: E.matmul(ps0[:, 0:256], ones_rowf[:1, :], rowb[:1, :],
                                             start=True, stop=True), deps=[h_nw, h_orf])
    h_nbc = em.op("vector", lambda E: E.tensor_copy(wbc_nw[:, :], ps0[:, 0:256]), deps=[hm0])
    h_a1 = em.op("scalar", lambda E: E.activation(alog_sb[:, 1:2], alog_sb[:, 0:1], AF.Exp),
                 deps=[h_al])
    h_a2 = em.op("vector", lambda E: E.tensor_scalar(
        alog_sb[:, 1:2], alog_sb[:, 1:2], -1.0, None, ALU.mult), deps=[h_a1])
    h_t0 = em.op("vector", lambda E: E.memset(tail1[:, :], 0.0))
    h_sv0 = []
    for h in range(3):
        h_sv0.append(em.op("vector", lambda E, h=h: E.memset(svs[h][:, :], 0.0)))
        h_sv0.append(em.op("vector", lambda E, h=h: E.memset(svb[h][:, :], 0.0)))

    h_xload = None
    h_ufm_last = None
    h_gst = None
    h_tailw = h_t0
    h_sl_prev = None
    h_qst = [None] * 12
    h_nst = [None] * 6
    h_ab_last = h_a2
    h_nsq_user = None
    h_qst_last = None
    h_nst_last = None
    h_cp_last = None
    for half in range(2):
        t0 = half * 2048
        for cc_ in range(16):
            tok = t0 + cc_ * 128
            h_x = em.dma("sync", x_sb[:, :], x_d[tok:tok + 128, :],
                         deps=[h_xload] if h_xload else [], chan="x1")
            co = cols[:, 0:8]
            hst_ = emit_ln_stats(L, x_sb[:, :], sq2[:, :], co[:, 0:4],
                                 deps=[h_x] + ([h_ufm_last] if h_ufm_last else []))
            hap = emit_ln_apply(L, x_sb[:, :], u_tm[:, :], co[:, 0:4],
                                deps=[hst_] + ([h_ufm_last] if h_ufm_last else []))
            h_xload = hap
            hp = hap
            for k in range(8):
                hmk = em.op("tensor", lambda E, k=k: E.transpose(
                    pt[:, :], u_tm[:, k * 128:(k + 1) * 128], ident[:, :]), deps=[hp, h_id])
                hcp = em.op("vector", lambda E, k=k, cc_=cc_: E.tensor_copy(
                    u_fm[k][:, cc_ * 128:(cc_ + 1) * 128], pt[:, :]), deps=[hmk])
                hp = hcp
            h_ufm_last = hp
            hgs = []
            for fh in range(2):
                def gmm(E, cc_=cc_, fh=fh):
                    ins = None
                    for k in range(8):
                        ins = E.matmul(
                            ps0[:, 128:512], u_fm[k][:, cc_ * 128:(cc_ + 1) * 128],
                            wvg_sb[:, k * 1536 + 768 + fh * 384:
                                   k * 1536 + 768 + (fh + 1) * 384],
                            start=(k == 0), stop=(k == 7))
                    return ins
                hmg = em.op("tensor", gmm, deps=[h_ufm_last, h_wvg] +
                            (hgs[-1:] if hgs else []) + ([h_gst] if h_gst else []))
                hgs.append(em.op("vector", lambda E, fh=fh: E.tensor_copy(
                    og_sb[:, fh * 384:(fh + 1) * 384], ps0[:, 128:512]),
                    deps=[hmg] + ([h_gst] if h_gst else [])))
            h_gst = em.dma("sync", gate_dr[tok:tok + 128, :], og_sb[:, :], deps=hgs,
                           chan="gst")
        for t5 in range(4):
            seg = t0 + t5 * 512
            def bmm(E, t5=t5):
                ins = None
                for k in range(8):
                    ins = E.matmul(ps0[0:3, 0:512], wab_sb[:, k * 6:k * 6 + 3],
                                   u_fm[k][:, t5 * 512:(t5 + 1) * 512],
                                   start=(k == 0), stop=(k == 7))
                return ins
            hmb = em.op("tensor", bmm, deps=[h_ufm_last, h_wab, h_gst, h_ab_last])
            hsg = em.op("scalar", lambda E, seg=seg: E.activation(
                be_fm[:, seg:seg + 512], ps0[0:3, 0:512], AF.Sigmoid), deps=[hmb])
            def amm(E, t5=t5):
                ins = None
                for k in range(8):
                    ins = E.matmul(ps0[0:3, 0:512], wab_sb[:, k * 6 + 3:k * 6 + 6],
                                   u_fm[k][:, t5 * 512:(t5 + 1) * 512],
                                   start=(k == 0), stop=(k == 7))
                return ins
            hma = em.op("tensor", amm, deps=[hsg])
            ha1 = em.op("vector", lambda E: E.tensor_scalar(
                convacc[0:3, 0:512], ps0[0:3, 0:512], dtb_sb[:, 0:1], None, ALU.add),
                deps=[hma, h_db, h_ab_last])
            ha2 = em.op("scalar", lambda E: E.activation(
                convacc[0:3, 512:1024], convacc[0:3, 0:512], AF.Exp), deps=[ha1])
            ha3 = em.op("scalar", lambda E, seg=seg: E.activation(
                a_fm[:, seg:seg + 512], convacc[0:3, 512:1024], AF.Ln, bias=1.0),
                deps=[ha2])
            h_ab_last = em.op("vector", lambda E, seg=seg: E.tensor_scalar(
                a_fm[:, seg:seg + 512], a_fm[:, seg:seg + 512], alog_sb[:, 1:2],
                None, ALU.mult), deps=[ha3, h_a2])
        for fc in range(12):
            wsb = wqk_sb if fc < 6 else wvg_sb
            wstride = 768 if fc < 6 else 1536
            wofs = fc * 128 if fc < 6 else (fc - 6) * 128
            h_cps = []
            for t5 in range(4):
                def xmm(E, fc=fc, t5=t5, wofs=wofs, wsb=wsb, wstride=wstride):
                    ins = None
                    for k in range(8):
                        ins = E.matmul(ps0[:, 0:512],
                                       wsb[:, k * wstride + wofs: k * wstride + wofs + 128],
                                       u_fm[k][:, t5 * 512:(t5 + 1) * 512],
                                       start=(k == 0), stop=(k == 7))
                    return ins
                ddx = [h_ufm_last, h_wqk, h_wvg, h_ab_last]
                if h_cps:
                    ddx.append(h_cps[-1])
                elif h_cp_last is not None:
                    ddx.append(h_cp_last)
                if h_nsq_user is not None:
                    ddx.append(h_nsq_user)
                hmx = em.op("tensor", xmm, deps=ddx)
                h_cps.append(em.op("vector", lambda E, t5=t5: E.tensor_copy(
                    convt[:, 3 + t5 * 512: 3 + (t5 + 1) * 512], ps0[:, 0:512]),
                    deps=[hmx] + ([h_tailw] if t5 == 0 else []) +
                    ([h_sl_prev] if h_sl_prev and t5 == 0 else [])))
            h_cp_last = h_cps[-1]
            hpre = em.op("vector", lambda E, fc=fc: E.tensor_copy(
                convt[:, 0:3], tail1[:, fc * 3:fc * 3 + 3]), deps=h_cps + [h_tailw])
            h_tailw = em.op("vector", lambda E, fc=fc: E.tensor_copy(
                tail1[:, fc * 3:fc * 3 + 3], convt[:, 2048:2051]), deps=[hpre])
            hc0 = em.op("vector", lambda E, fc=fc: E.tensor_scalar(
                convacc[:, :], convt[:, 0:2048], convw_sb[:, fc * 4:fc * 4 + 1],
                None, ALU.mult), deps=[hpre, h_cw, h_tailw, h_ab_last] +
                ([h_sl_prev] if h_sl_prev else []))
            for j in range(1, 4):
                hc0 = em.op("vector", lambda E, fc=fc, j=j: E.scalar_tensor_tensor(
                    convacc[:, :], convt[:, j:j + 2048], convw_sb[:, fc * 4 + j:fc * 4 + j + 1],
                    convacc[:, :], ALU.mult, ALU.add), deps=[hc0])
            hsl = em.op("scalar", lambda E: E.activation(
                qkvh[:, :], convacc[:, :], AF.Silu),
                deps=[hc0] + ([h_qst[fc]] if h_qst[fc] else []) +
                ([h_qst_last] if h_qst_last else []) +
                ([h_sl_prev] if h_sl_prev else []))
            h_sl_prev = hsl
            h_qst[fc] = em.dma("sync", qkv_dr[fc * 128:(fc + 1) * 128, t0:t0 + 2048],
                               qkvh[:, :], deps=[hsl], chan="qst")
            h_qst_last = h_qst[fc]
            if fc < 6:
                hsq = em.op("scalar", lambda E: E.activation(qsq[:, :], qkvh[:, :],
                                                             AF.Square),
                            deps=[hsl] + ([h_nsq_user] if h_nsq_user else []))
                hnc = None
                for t5 in range(4):
                    hcm = em.op("tensor", lambda E, t5=t5: E.matmul(
                        ps0[0:1, 0:512], ones_col[:, 0:1], qsq[:, t5 * 512:(t5 + 1) * 512],
                        start=True, stop=True),
                        deps=[hsq, h_oc] + ([hnc] if hnc else []) +
                        ([h_nst[fc]] if h_nst[fc] else []))
                    hnc = em.op("vector", lambda E, t5=t5: E.tensor_copy(
                        nrow[:1, t5 * 512:(t5 + 1) * 512], ps0[0:1, 0:512]),
                        deps=[hcm] + ([h_nst_last] if h_nst_last and t5 == 0 else []))
                h_nst[fc] = em.dma("sync", nsq_dr[fc:fc + 1, t0:t0 + 2048], nrow[:1, :],
                                   deps=[hnc], chan="nst")
                h_nst_last = h_nst[fc]
                h_nsq_user = hnc

    hcum = h_ab_last
    for ch in range(NCH):
        hcum = em.op("vector", lambda E, ch=ch: E.tensor_tensor_scan(
            cum[:, ch * C:(ch + 1) * C], a_fm[:, ch * C:(ch + 1) * C],
            a_fm[:, ch * C:(ch + 1) * C], 0.0, ALU.add, ALU.bypass), deps=[hcum])
    hfp = em.op("scalar", lambda E: E.activation(fp_fm[:, :], cum[:, :], AF.Exp),
                deps=[hcum])
    hgc0 = em.op("vector", lambda E: E.tensor_copy(
        gcols[:, :], cum.rearrange("p (c t) -> p c t", t=C)[:, :, C - 1:C]), deps=[hcum])
    hgc = em.op("scalar", lambda E: E.activation(gcols[:, :], gcols[:, :], AF.Exp),
                deps=[hgc0, hfp])
    hT = em.op("tensor", lambda E: E.transpose(ptf[0:NCH, 0:3], gcols[:, :],
                                               identf[0:3, 0:3]), deps=[hgc, h_idf])
    hTc = em.op("vector", lambda E: E.tensor_copy(gcT[:, :], ptf[0:NCH, 0:3]), deps=[hT])
    hgd = em.dma("sync", gam_dr[:, :], gcT[:, :], deps=[hTc], chan="gamd")
    hcd = em.dma("sync", cum_dr[:, :], cum[:, :], deps=[hcum, hgc0], chan="cumd")
    hfd = em.dma("sync", fp_dr[:, :], fp_fm[:, :], deps=[hfp], chan="fpd")
    hbd = em.dma("sync", br_dr[:, :], be_fm[:, :], deps=[h_ab_last, hcum], chan="brd")

    h_crow = None
    h_frow = None
    h_brw = None
    h_grow = None
    h_nqk = None
    h_gate = None
    h_ogst = None
    h_qf = None
    h_kf = None
    h_vf = None
    hTl = hT
    hVl = hTc
    hSl = hgc
    h_og_w = [None, None, None]
    for ci in range(NCH):
        tok = ci * C
        dd = [hTl, hVl, hSl]
        h_crow = em.dma("sync", crow[:1, :].rearrange("p (h t) -> p h t", h=3),
                        cum_dr.ap()[:, tok:tok + C],
                        deps=([h_crow] if h_crow else []) + [hcd] + dd, chan="crowl")
        h_frow = em.dma("sync", frow[:1, :].rearrange("p (h t) -> p h t", h=3),
                        fp_dr.ap()[:, tok:tok + C],
                        deps=([h_frow] if h_frow else []) + [hfd] + dd, chan="frowl")
        h_brw = em.dma("sync", brw[:1, :].rearrange("p (h t) -> p h t", h=3),
                       br_dr.ap()[:, tok:tok + C],
                       deps=([h_brw] if h_brw else []) + [hbd] + dd, chan="brwl")
        h_grow = em.dma("sync", grow[:1, :], gam_dr[ci:ci + 1, :],
                        deps=([h_grow] if h_grow else []) + [hgd] + dd, chan="growl")
        h_nqk = em.dma("sync", nqk[:1, :].rearrange("p (h t) -> p h t", h=6),
                       nsq_dr.ap()[:, tok:tok + C],
                       deps=([h_nqk] if h_nqk else []) + [h_nst[5]] + dd, chan="nqkl")
        h_gate = em.dma("sync", gate_sb[:, :], gate_dr[tok:tok + C, :],
                        deps=([h_gate] if h_gate else []) + [h_gst] + dd, chan="gatel")
        hn1 = em.op("vector", lambda E: E.tensor_scalar(
            nrst[:1, :], nqk[:1, :], 1.0, 1e-12, ALU.mult, ALU.add), deps=[h_nqk] + dd)
        hn2 = em.op("scalar", lambda E: E.activation(nqk[:1, :], nrst[:1, :], AF.Sqrt),
                    deps=[hn1])
        hn3 = em.op("vector", lambda E: E.reciprocal(nrst[:1, :], nqk[:1, :]), deps=[hn2])
        hn4 = em.op("vector", lambda E: E.tensor_scalar(
            nrst[:1, 0:384], nrst[:1, 0:384], 0.08838834764831845, None, ALU.mult),
            deps=[hn3])
        hm1_ = em.op("tensor", lambda E, tok=tok: E.transpose(
            ptf[:, 0:3], cum[:, tok:tok + C], identf[32:35, 32:35]),
            deps=[hcum, h_idf] + dd)
        hct = em.op("vector", lambda E: E.tensor_copy(cum_tm[:, 0:3], ptf[:, 0:3]),
                    deps=[hm1_] + dd)
        hm2_ = em.op("tensor", lambda E, tok=tok: E.transpose(
            pt[:, 0:3], be_fm[:, tok:tok + C], ident[32:35, 32:35]),
            deps=[h_ab_last, h_id, hct] + dd)
        hbt = em.op("vector", lambda E: E.tensor_copy(b_tmc[:, 0:3], pt[:, 0:3]),
                    deps=[hm2_] + dd)
        hm3_ = em.op("tensor", lambda E: E.matmul(
            ptf[:, 4:7], ones_rowf[:1, :], grow[:1, :], start=True, stop=True),
            deps=[h_grow, h_orf, hct] + dd)
        hgbc = em.op("vector", lambda E: E.tensor_copy(gbc[:, 0:3], ptf[:, 4:7]),
                     deps=[hm3_, hbt] + dd)
        hTl, hVl, hSl = hm3_, hgbc, hn2
        for h in range(3):
            hd = [hTl, hVl, hSl]
            h_qf = em.dma("sync", qfm_c[:, :], qkv_dr[h * 128:(h + 1) * 128, tok:tok + C],
                          deps=([h_qf] if h_qf else []) + [h_qst[h]] + hd, chan="qf")
            h_kf = em.dma("sync", kfm_c[:, :],
                          qkv_dr[(3 + h) * 128:(4 + h) * 128, tok:tok + C],
                          deps=([h_kf] if h_kf else []) + [h_qst[3 + h]] + hd, chan="kf")
            h_vf0 = em.dma("sync", vfm_c[:, 0:128],
                           qkv_dr[(6 + 2 * h) * 128:(7 + 2 * h) * 128, tok:tok + C],
                           deps=([h_vf] if h_vf else []) + [h_qst[6 + 2 * h]] + hd,
                           chan="vf")
            h_vf = em.dma("sync", vfm_c[:, 128:256],
                          qkv_dr[(7 + 2 * h) * 128:(8 + 2 * h) * 128, tok:tok + C],
                          deps=[h_vf0, h_qst[7 + 2 * h]], chan="vf")
            hb1 = em.op("tensor", lambda E, h=h: E.matmul(
                psG2[:, :], ones_rowf[:1, :], nrst[:1, h * 128:(h + 1) * 128],
                start=True, stop=True), deps=[hn4, h_orf] + hd)
            hcq = em.op("vector", lambda E: E.tensor_copy(gmq[:, :], psG2[:, :]),
                        deps=[hb1] + hd)
            hqn = em.op("vector", lambda E: E.tensor_mul(qfm_c[:, :], qfm_c[:, :],
                                                         gmq[:, :]), deps=[hcq, h_qf])
            hb2 = em.op("tensor", lambda E, h=h: E.matmul(
                psG2[:, :], ones_rowf[:1, :], nrst[:1, 384 + h * 128:384 + (h + 1) * 128],
                start=True, stop=True), deps=[hn4, h_orf, hcq])
            hck = em.op("vector", lambda E: E.tensor_copy(gmq[:, :], psG2[:, :]),
                        deps=[hb2, hqn])
            hkn = em.op("vector", lambda E: E.tensor_mul(kfm_c[:, :], kfm_c[:, :],
                                                         gmq[:, :]), deps=[hck, h_kf])
            hb3 = em.op("tensor", lambda E, h=h: E.matmul(
                psG2[:, :], ones_row[:1, :], frow[:1, h * 128:(h + 1) * 128],
                start=True, stop=True), deps=[h_frow, h_or, hbt, hb2, hck] + hd)
            hqg = em.op("vector", lambda E: E.tensor_mul(qg[:, :], psG2[:, :],
                                                         qfm_c[:, :]),
                        deps=[hb3, hqn] + hd)
            hg1 = em.op("tensor", lambda E: E.matmul(
                psG[:, :], kfm_c[:, :], qfm_c[:, :], start=True, stop=True),
                deps=[hqn, hkn] + hd)
            hgm1 = em.op("vector", lambda E: E.tensor_mul(gm1[:, :], psG[:, :],
                                                          triuI[:, :]),
                         deps=[hg1, h_tui] + hd)
            hg2 = em.op("tensor", lambda E: E.matmul(
                psG2[:, :], kfm_c[:, :], kfm_c[:, :], start=True, stop=True),
                deps=[hkn, hck, hqg] + hd)
            hgm2 = em.op("vector", lambda E: E.tensor_mul(gm2[:, :], psG2[:, :],
                                                          triuSn[:, :]),
                         deps=[hg2, h_tsn, hgm1] + hd)
            hb4 = em.op("tensor", lambda E, h=h: E.matmul(
                psG[:, :], ones_rowf[:1, :], crow[:1, h * 128:(h + 1) * 128],
                start=True, stop=True), deps=[h_crow, h_orf, hgm1] + hd)
            hd1 = em.op("vector", lambda E, h=h: E.tensor_scalar(
                dsl1[:, :], psG[:, :], cum_tm[:, h:h + 1], None, ALU.subtract),
                deps=[hb4, hct] + hd)
            hd2 = em.op("vector", lambda E: E.tensor_scalar(
                dsl1[:, :], dsl1[:, :], 0.0, None, ALU.min), deps=[hd1])
            hd3 = em.op("scalar", lambda E: E.activation(eF[:, :], dsl1[:, :], AF.Exp),
                        deps=[hd2] + hd)
            hd4 = em.op("vector", lambda E: E.tensor_copy(eT[:, :], eF[:, :]),
                        deps=[hd3] + hd)
            hat = em.op("vector", lambda E: E.tensor_mul(amat[:, :], eT[:, :], gm1[:, :]),
                        deps=[hd4, hgm1] + hd)
            hb5 = em.op("tensor", lambda E, h=h: E.matmul(
                psG2[:, :], ones_row[:1, :], brw[:1, h * 128:(h + 1) * 128],
                start=True, stop=True), deps=[h_brw, h_or, hqg, hgm2] + hd)
            hq1 = em.op("vector", lambda E: E.tensor_mul(gmq[:, :], psG2[:, :],
                                                         gm2[:, :]),
                        deps=[hb5, hgm2, hkn] + hd)
            hnt = em.op("vector", lambda E: E.tensor_mul(bnxT[:, :], eT[:, :], gmq[:, :]),
                        deps=[hq1, hd4] + hd)
            hm4_ = em.op("tensor", lambda E: E.transpose(pt[:, :], bnxT[:, :], ident[:, :]),
                         deps=[hnt, h_id, hq1] + hd)
            hnn = em.op("vector", lambda E: E.tensor_copy(bnx[:, :], pt[:, :]),
                        deps=[hm4_] + hd)
            hmm = em.op("vector", lambda E: E.tensor_add(mmt[:, :], bnxT[:, :],
                                                         ident[:, :]),
                        deps=[hnt, h_id, hnn] + hd)
            hbx, hbxT, hmt = hnn, hnt, hmm
            for it in range(6):
                hs1 = em.op("tensor", lambda E: E.matmul(
                    psG[:, :], bnxT[:, :], bnx[:, :], start=True, stop=True),
                    deps=[hbx, hbxT, hd1, hmt] + hd)
                hs2 = None
                if it < 5:
                    hs2 = em.op("tensor", lambda E: E.matmul(
                        psG2[:, :], bnx[:, :], bnxT[:, :], start=True, stop=True),
                        deps=[hbx, hbxT] + hd)
                hcx = em.op("vector", lambda E: E.tensor_copy(bnx[:, :], psG[:, :]),
                            deps=[hs1] + ([hs2] if hs2 else []) + hd)
                hcxT = hbxT
                if it < 5:
                    hcxT = em.op("vector", lambda E: E.tensor_copy(bnxT[:, :], psG2[:, :]),
                                 deps=[hs2, hcx] + hd)
                hs3 = em.op("tensor", lambda E: E.matmul(
                    psG[:, :], bnx[:, :], mmt[:, :], start=True, stop=True),
                    deps=[hcx, hmt] + hd)
                hmt = em.op("vector", lambda E: E.tensor_add(mmt[:, :], mmt[:, :],
                                                             psG[:, :]),
                            deps=[hs3, hcxT] + hd)
                hbx, hbxT = hcx, hcxT
            hcr = em.op("tensor", lambda E, h=h: E.matmul(
                pU[:, :], kfm_c[:, :], svb[h][:, :], start=True, stop=True),
                deps=[hkn, h_sv0[2 * h + 1], hg2] + hd)
            hgx = em.op("scalar", lambda E, h=h: E.activation(
                gcolx[:, 0:1], cum_tm[:, h:h + 1], AF.Exp), deps=[hct, hd3] + hd)
            hgy = em.op("vector", lambda E, h=h: E.tensor_mul(
                gcolx[:, 1:2], b_tmc[:, h:h + 1], gcolx[:, 0:1]), deps=[hgx, hbt] + hd)
            hoc = em.op("vector", lambda E: E.tensor_scalar(
                oc[:, :], pU[:, :], gcolx[:, 1:2], None, ALU.mult),
                deps=[hcr, hgy] + hd)
            hv1 = em.op("tensor", lambda E: E.transpose(pt[:, :], vfm_c[:, 0:128],
                                                        ident[:, :]),
                        deps=[h_vf, h_id, hnn, hm4_] + hd)
            hv2 = em.op("vector", lambda E: E.tensor_copy(v_tm[:, 0:128], pt[:, :]),
                        deps=[hv1] + hd)
            hv3 = em.op("tensor", lambda E: E.transpose(pt[:, :], vfm_c[:, 128:256],
                                                        ident[:, :]), deps=[hv2] + hd)
            hv4 = em.op("vector", lambda E: E.tensor_copy(v_tm[:, 128:256], pt[:, :]),
                        deps=[hv3] + hd)
            hbb = em.op("vector", lambda E, h=h: E.scalar_tensor_tensor(
                b_sb[:, :], v_tm[:, :], b_tmc[:, h:h + 1], oc[:, :],
                ALU.mult, ALU.subtract), deps=[hv4, hoc, hbt] + hd)
            hu1 = em.op("tensor", lambda E: E.matmul(
                pU[:, :], mmt[:, :], b_sb[:, :], start=True, stop=True),
                deps=[hmt, hbb, hoc] + hd)
            hu2 = em.op("vector", lambda E: E.tensor_copy(u_sb[:, :], pU[:, :]),
                        deps=[hu1] + hd)
            def omm(E, h=h):
                i1 = E.matmul(pO[:, :], qg[:, :], svb[h][:, :], start=True, stop=False)
                i2 = E.matmul(pO[:, :], amat[:, :], u_sb[:, :], start=False, stop=True)
                return [i1, i2]
            hO = em.op("tensor", omm, deps=[hqg, hu2, hat, h_sv0[2 * h + 1]] + hd)
            hp1 = em.op("scalar", lambda E: E.activation(oc[:, :], pO[:, :], AF.Square),
                        deps=[hO, hbb] + hd)
            hp2 = em.op("vector", lambda E: E.reduce_sum(
                gcolx[:, 2:3], oc[:, :], axis=mybir.AxisListType.X), deps=[hp1] + hd)
            hp3 = em.op("vector", lambda E: E.tensor_scalar(
                gcolx[:, 3:4], gcolx[:, 2:3], 1.0 / 256.0, 1e-5, ALU.mult, ALU.add),
                deps=[hp2] + hd)
            hp4 = em.op("scalar", lambda E: E.activation(gcolx[:, 4:5], gcolx[:, 3:4],
                                                         AF.Sqrt), deps=[hp3] + hd)
            hp5 = em.op("vector", lambda E: E.reciprocal(gcolx[:, 5:6], gcolx[:, 4:5]),
                        deps=[hp4] + hd)
            hp6 = em.op("vector", lambda E: E.tensor_scalar(
                oc[:, :], pO[:, :], gcolx[:, 5:6], None, ALU.mult), deps=[hp5, hp1] + hd)
            hp7 = em.op("vector", lambda E: E.tensor_mul(oc[:, :], oc[:, :], wbc_nw[:, :]),
                        deps=[hp6, h_nbc] + hd)
            hp8 = em.op("scalar", lambda E, h=h: E.activation(
                oc2[:, :], gate_sb[:, h * 256:(h + 1) * 256], AF.Silu),
                deps=[h_gate, hp4] + hd)
            hp9 = em.op("vector", lambda E, h=h: E.tensor_mul(
                og_sb[:, h * 256:(h + 1) * 256], oc[:, :], oc2[:, :]),
                deps=[hp7, hp8] + ([h_ogst] if h_ogst else []) +
                ([h_og_w[h]] if h_og_w[h] else []) + hd)
            h_og_w[h] = hp9
            hk1 = em.op("tensor", lambda E: E.transpose(pt[:, :], kfm_c[:, :],
                                                        ident[:, :]),
                        deps=[hkn, h_id, hv4, hO] + hd)
            hk2 = em.op("vector", lambda E: E.tensor_scalar(
                kgb[:, :], pt[:, :], eF[:, C - 1:C], None, ALU.mult),
                deps=[hk1, hd3] + hd)
            hs4 = em.op("tensor", lambda E: E.matmul(
                pS[:, :], kgb[:, :], u_sb[:, :], start=True, stop=True),
                deps=[hk2, hu2, hk1] + hd)
            hz1 = em.op("vector", lambda E, h=h: E.tensor_scalar(
                svs[h][:, :], svs[h][:, :], gbc[:, h:h + 1], None, ALU.mult),
                deps=[hgbc, hcr, hO, h_sv0[2 * h]] + hd)
            hz2 = em.op("vector", lambda E, h=h: E.tensor_add(
                svs[h][:, :], svs[h][:, :], pS[:, :]), deps=[hz1, hs4] + hd)
            hz3 = em.op("vector", lambda E, h=h: E.tensor_copy(
                svb[h][:, :], svs[h][:, :]), deps=[hz2] + hd)
            h_sv0[2 * h], h_sv0[2 * h + 1] = hz2, hz3
            hTl, hVl, hSl = hs4, hz3, hp8
        h_ogst = em.dma("sync", og_d[tok:tok + C, :], og_sb[:, :],
                        deps=[h_og_w[0], h_og_w[1], h_og_w[2]] +
                        ([h_ogst] if h_ogst else []), chan="ogst")
    for e in ENGS:
        em.barrier(e, [h_ogst])
    em.emit()
    return nc



def prep_host(inputs):
    p = {}
    g = inputs
    p["x"] = np.ascontiguousarray(g["L0_attn_x"])

    p["l0_in_proj_T"] = np.ascontiguousarray(g["L0_attn_in_proj_w"].T)
    p["l0_out_proj_T"] = np.ascontiguousarray(g["L0_attn_out_proj_w"].T)
    p["l0_gate_T"] = np.ascontiguousarray(g["L0_attn_gate_w"].T)
    p["l0_up_T"] = np.ascontiguousarray(g["L0_attn_up_w"].T)
    p["l0_down_T"] = np.ascontiguousarray(g["L0_attn_down_w"].T)
    p["l0_conv_w"] = np.ascontiguousarray(g["L0_attn_conv_w"])
    p["l0_conv_b"] = np.ascontiguousarray(g["L0_attn_conv_b"])
    p["l0_dt_bias"] = np.ascontiguousarray(g["L0_attn_dt_bias"])
    p["l0_A_log"] = np.ascontiguousarray(g["L0_attn_A_log"])
    p["l0_D"] = np.ascontiguousarray(g["L0_attn_D_param"])
    p["l0_ssm_norm_w"] = np.ascontiguousarray(g["L0_attn_ssm_norm_w"])
    p["l0_prenorm_w"] = np.ascontiguousarray(g["L0_attn_prenorm_w"])
    p["l0_postnorm_w"] = np.ascontiguousarray(g["L0_attn_postnorm_w"])

    E, R, HS, FF = 8, 64, 1024, 4096
    p["moe_w1A_T"] = np.ascontiguousarray(
        g["L0_ffn_w1_A"].reshape(E, R, HS).transpose(0, 2, 1))
    p["moe_w1B"] = np.ascontiguousarray(g["L0_ffn_w1_B"].reshape(E, R, FF))
    p["moe_w2A_T"] = np.ascontiguousarray(
        g["L0_ffn_w2_A"].reshape(E, R, FF).transpose(0, 2, 1))
    p["moe_w2B"] = np.ascontiguousarray(g["L0_ffn_w2_B"].reshape(E, R, HS))
    p["moe_ew"] = np.ascontiguousarray(g["L0_ffn_expert_weights"])
    p["moe_ei"] = np.ascontiguousarray(g["L0_ffn_expert_indices"])

    for k in ("Wq", "Wk", "Wv", "Wb", "Wa", "Wg", "Wo"):
        p[f"l1_{k}"] = np.ascontiguousarray(g[f"L1_attn_{k}"])
    p["l1_A_log"] = np.ascontiguousarray(g["L1_attn_A_log"])
    p["l1_dt_bias"] = np.ascontiguousarray(g["L1_attn_dt_bias"])
    p["l1_conv_q"] = np.ascontiguousarray(g["L1_attn_conv_q"])
    p["l1_conv_k"] = np.ascontiguousarray(g["L1_attn_conv_k"])
    p["l1_conv_v"] = np.ascontiguousarray(g["L1_attn_conv_v"])
    p["l1_norm_w"] = np.ascontiguousarray(g["L1_attn_norm_w"])
    for k in ("U1", "V1", "b1", "U2", "V2", "b2"):
        p[f"l1f_{k}"] = np.ascontiguousarray(g[f"L1_ffn_{k}"])

    p["l2_W_in"] = np.ascontiguousarray(g["L2_attn_W_in"])
    p["l2_W_A"] = np.ascontiguousarray(g["L2_attn_W_A"])
    p["l2_W_B"] = np.ascontiguousarray(g["L2_attn_W_B"])
    p["l2_W_C"] = np.ascontiguousarray(g["L2_attn_W_C"])
    p["l2_W_out"] = np.ascontiguousarray(g["L2_attn_W_out"])
    p["l2_norm_w"] = np.ascontiguousarray(g["L2_attn_norm_w"])
    p["l2f_clip_x"] = np.ascontiguousarray(g["L2_ffn_clip_x"])
    p["l2f_clip_down"] = np.ascontiguousarray(g["L2_ffn_clip_down"])
    p["l2f_gate_T"] = np.ascontiguousarray(g["L2_ffn_w_gate"].T)
    p["l2f_up_T"] = np.ascontiguousarray(g["L2_ffn_w_up"].T)
    p["l2f_down_T"] = np.ascontiguousarray(g["L2_ffn_w_down"].T)

    p["l3_pq"] = np.ascontiguousarray(g["L3_attn_proj_q"])
    p["l3_pk"] = np.ascontiguousarray(g["L3_attn_proj_k"])
    p["l3_pv"] = np.ascontiguousarray(g["L3_attn_proj_v"])
    p["l3_po_T"] = np.ascontiguousarray(g["L3_attn_proj_o"].transpose(0, 2, 1))
    p["l3f_wq_T"] = np.ascontiguousarray(g["L3_ffn_weight_q"].T)
    p["l3f_wscale"] = np.ascontiguousarray(g["L3_ffn_weight_scale"])
    p["l3f_bias"] = np.ascontiguousarray(g["L3_ffn_bias"])
    return p



_NC_CACHE = {}


def _nc(name, builder):
    if name not in _NC_CACHE:
        _NC_CACHE[name] = builder()
    return _NC_CACHE[name]


def kernel(**inputs):
    p = prep_host({k: np.asarray(v) for k, v in inputs.items()})
    T, NT = 2048, 16384
    cores = list(range(8))

    def rows(a, c):
        return np.ascontiguousarray(a[c * T:(c + 1) * T])

    xfull = np.ascontiguousarray(p["x"].reshape(NT, 1024))
    ipT = p["l0_in_proj_T"]
    ins = []
    for c in cores:
        b, g = c // 2, c % 2
        cw = np.concatenate([p["l0_conv_w"][g * 1024:(g + 1) * 1024],
                             p["l0_conv_w"][2048:2304]], 0)
        cb = np.concatenate([p["l0_conv_b"][g * 1024:(g + 1) * 1024],
                             p["l0_conv_b"][2048:2304]], 0)
        ins.append({
            "xb": xfull[b * 4096:(b + 1) * 4096],
            "prew": p["l0_prenorm_w"].reshape(1, -1),
            "winz": np.ascontiguousarray(ipT[:, g * 1024:(g + 1) * 1024]),
            "winxbc": np.ascontiguousarray(np.concatenate(
                [ipT[:, 2048 + g * 1024:2048 + (g + 1) * 1024],
                 ipT[:, 4096:4352]], axis=1)),
            "windt": np.ascontiguousarray(ipT[:, 4352 + 16 * g:4352 + 16 * g + 16]),
            "convw": np.ascontiguousarray(
                cw.reshape(10, 128, 4).transpose(1, 0, 2).reshape(128, 40)),
            "convb": np.ascontiguousarray(cb.reshape(10, 128).T),
            "dtb": np.ascontiguousarray(p["l0_dt_bias"][16 * g:16 * g + 16].reshape(16, 1)),
            "alog": np.ascontiguousarray(p["l0_A_log"][16 * g:16 * g + 16].reshape(16, 1)),
            "drep": np.ascontiguousarray(
                np.repeat(p["l0_D"][16 * g:16 * g + 16], 64).reshape(1, 1024)),
        })
    res = run_bass_kernel_spmd(_nc("l0a", build_l0a), ins, core_ids=cores)
    yg = np.empty((NT, 2048), BF)
    for c in cores:
        b, g = c // 2, c % 2
        yg[b * 4096:(b + 1) * 4096, g * 1024:(g + 1) * 1024] = res.results[c]["ygh"]

    ins = [{"x": rows(xfull, c), "yg": rows(yg, c),
            "opw": p["l0_out_proj_T"], "gw": p["l0_gate_T"], "uw": p["l0_up_T"],
            "dw": p["l0_down_T"], "ssmw": p["l0_ssm_norm_w"].reshape(1, -1),
            "postw": p["l0_postnorm_w"].reshape(1, -1)} for c in cores]
    res = run_bass_kernel_spmd(_nc("l0b", build_l0b), ins, core_ids=cores)
    x1 = np.concatenate([r["x1"] for r in res.results], axis=0)

    eiT = np.full((2, NT), 8, np.int32)
    ewT = np.zeros((2, NT), np.float32)
    eiT[:, :8192] = p["moe_ei"].T
    ewT[:, :8192] = p["moe_ew"].T
    ins = [{"x1": rows(x1, c),
            "eiT": np.ascontiguousarray(eiT[:, c * T:(c + 1) * T]),
            "ewT": np.ascontiguousarray(ewT[:, c * T:(c + 1) * T]),
            "w1a": p["moe_w1A_T"], "w1b": p["moe_w1B"], "w2a": p["moe_w2A_T"],
            "w2b": np.ascontiguousarray(p["moe_w2B"].reshape(512, 1024))} for c in cores]
    res = run_bass_kernel_spmd(_nc("l0ffn", build_l0ffn), ins, core_ids=cores)
    x2 = np.concatenate([r["x2"] for r in res.results], axis=0)

    ins = []
    for c in cores:
        b, g = c // 2, c % 2
        cw = np.concatenate([p["l1_conv_q"][g * 384:(g + 1) * 384],
                             p["l1_conv_k"][g * 384:(g + 1) * 384],
                             p["l1_conv_v"][g * 768:(g + 1) * 768]], 0)
        ins.append({
            "xb": np.ascontiguousarray(x2[b * 4096:(b + 1) * 4096]),
            "wqk": np.ascontiguousarray(np.concatenate(
                [p["l1_Wq"][:, g * 384:(g + 1) * 384],
                 p["l1_Wk"][:, g * 384:(g + 1) * 384]], 1)),
            "wvg": np.ascontiguousarray(np.concatenate(
                [p["l1_Wv"][:, g * 768:(g + 1) * 768],
                 p["l1_Wg"][:, g * 768:(g + 1) * 768]], 1)),
            "wab": np.ascontiguousarray(np.concatenate(
                [p["l1_Wb"][:, g * 3:g * 3 + 3], p["l1_Wa"][:, g * 3:g * 3 + 3]], 1)),
            "convw": np.ascontiguousarray(
                cw.reshape(12, 128, 4).transpose(1, 0, 2).reshape(128, 48)),
            "dtb3": np.ascontiguousarray(p["l1_dt_bias"][g * 3:g * 3 + 3].reshape(3, 1)),
            "alog3": np.ascontiguousarray(p["l1_A_log"][g * 3:g * 3 + 3].reshape(3, 1)),
            "nwr": np.ascontiguousarray(p["l1_norm_w"].reshape(1, 256)),
        })
    res = run_bass_kernel_spmd(_nc("l1a", build_l1a), ins, core_ids=cores)
    ogn = np.empty((NT, 1536), BF)
    for c in cores:
        b, g = c // 2, c % 2
        ogn[b * 4096:(b + 1) * 4096, g * 768:(g + 1) * 768] = res.results[c]["ogn"]

    ins = [{"x2": rows(x2, c), "ogn": rows(ogn, c),
            "wo": p["l1_Wo"], "u1": p["l1f_U1"], "v1": p["l1f_V1"],
            "b1c": np.ascontiguousarray(p["l1f_b1"].reshape(32, 128).T),
            "u2": p["l1f_U2"], "v2": p["l1f_V2"],
            "b2r": p["l1f_b2"].reshape(1, -1)} for c in cores]
    res = run_bass_kernel_spmd(_nc("l1b", build_l1b), ins, core_ids=cores)
    x4 = np.concatenate([r["x4"] for r in res.results], axis=0)

    wabc = np.ascontiguousarray(np.concatenate(
        [p["l2_W_A"], p["l2_W_B"], p["l2_W_C"]], axis=1))
    ins = [{"x4": rows(x4, c), "win": p["l2_W_in"], "wabc": wabc} for c in cores]
    res = run_bass_kernel_spmd(_nc("l2a1", build_l2a1), ins, core_ids=cores)
    xin = np.concatenate([r["xin"] for r in res.results], axis=0)
    abc = np.concatenate([r["abc"] for r in res.results], axis=0)

    ins = []
    for c in cores:
        b, g = c // 2, c % 2
        ins.append({"xinc": np.ascontiguousarray(
                        xin[b * 4096:(b + 1) * 4096, g * 1024:(g + 1) * 1024]),
                    "abc": np.ascontiguousarray(abc[b * 4096:(b + 1) * 4096])})
    res = run_bass_kernel_spmd(_nc("l2a2", build_l2a2), ins, core_ids=cores)
    osc = np.empty((NT, 2048), np.float32)
    for c in cores:
        b, g = c // 2, c % 2
        osc[b * 4096:(b + 1) * 4096, g * 1024:(g + 1) * 1024] = res.results[c]["osc"]

    ins = [{"x4": rows(x4, c), "osc": rows(osc, c),
            "wout": p["l2_W_out"], "nwr": p["l2_norm_w"].reshape(1, -1)} for c in cores]
    res = run_bass_kernel_spmd(_nc("l2b", build_l2b), ins, core_ids=cores)
    x5 = np.concatenate([r["x5"] for r in res.results], axis=0)
    xmax8 = np.ascontiguousarray(np.stack(
        [res.results[c]["xmax"].reshape(()) for c in cores]).reshape(1, 8))

    ins = [{"x5": rows(x5, c), "xmax8": xmax8,
            "clipx": p["l2f_clip_x"].reshape(1, 1),
            "gwq": p["l2f_gate_T"], "uwq": p["l2f_up_T"]} for c in cores]
    res = run_bass_kernel_spmd(_nc("l2f1", build_l2f1), ins, core_ids=cores)
    hbuf = np.concatenate([r["hout"] for r in res.results], axis=0)
    hmax8 = np.ascontiguousarray(np.stack(
        [res.results[c]["hmax"].reshape(()) for c in cores]).reshape(1, 8))

    ins = [{"x5": rows(x5, c), "hin": rows(hbuf, c), "hmax8": hmax8,
            "clipd": p["l2f_clip_down"].reshape(1, 1),
            "dwq": p["l2f_down_T"]} for c in cores]
    res = run_bass_kernel_spmd(_nc("l2f2", build_l2f2), ins, core_ids=cores)
    x6 = np.concatenate([r["x6"] for r in res.results], axis=0)

    ins = []
    for c in cores:
        b, g = c // 2, c % 2
        ins.append({"xb": np.ascontiguousarray(x6[b * 4096:(b + 1) * 4096]),
                    "pq4": np.ascontiguousarray(p["l3_pq"][g * 4:(g + 1) * 4]),
                    "pk": p["l3_pk"], "pv": p["l3_pv"]})
    res = run_bass_kernel_spmd(_nc("l3a", build_l3a), ins, core_ids=cores)
    l3o = np.empty((NT, 1024), BF)
    for c in cores:
        b, g = c // 2, c % 2
        l3o[b * 4096:(b + 1) * 4096, g * 512:(g + 1) * 512] = res.results[c]["ofm"].T
    l3o = np.ascontiguousarray(l3o)

    ins = [{"x6": rows(x6, c), "orow": rows(l3o, c),
            "po": np.ascontiguousarray(p["l3_po_T"].reshape(1024, 1024)),
            "wqi": p["l3f_wq_T"], "wscr": p["l3f_wscale"].reshape(1, -1),
            "biasr": p["l3f_bias"].reshape(1, -1)} for c in cores]
    res = run_bass_kernel_spmd(_nc("l3b", build_l3b), ins, core_ids=cores)
    x8 = np.concatenate([r["x8"] for r in res.results], axis=0)

    return np.ascontiguousarray(x8.reshape(4, 4096, 1024).astype(np.float32))
